# version 11
# speedup vs baseline: 1.0651x; 1.0173x over previous
import contextlib
import numpy as np
import ml_dtypes
import concourse.bass as bass
import concourse.mybir as mybir
from concourse.bass_utils import run_bass_kernel_spmd

F32 = mybir.dt.float32
BF16 = mybir.dt.bfloat16
F16 = mybir.dt.float16
I32 = mybir.dt.int32
U16 = mybir.dt.uint16
AF = mybir.ActivationFunctionType
ALU = mybir.AluOpType
AX = mybir.AxisListType

D = 2048
T = 4096
B = 2
TC = 1024
NH = 16
HD = 128
HPC = 4
SCALE = 0.08838834764831845
NEG = -1.0e30
RG = [[0, 1, 2, 3], [4, 5, 6, 7]]
NQC = T // 512
NTT = TC // 128

bf = ml_dtypes.bfloat16


class Prog:
    ENGINES = ("tensor", "vector", "scalar", "gpsimd", "sync")
    SEMKEY = {"tensor": "T", "vector": "V", "scalar": "S", "gpsimd": "G"}
    DMAKEY = {"sync": "dS", "scalar": "dA", "gpsimd": "dG", "vector": "dV"}

    NDMASEM = 12

    def __init__(self):
        self.items = {e: [] for e in self.ENGINES}
        self.cnt = {k: 0 for k in ("T", "V", "S", "G", "C")}
        for i in range(self.NDMASEM):
            self.cnt[f"DH{i}"] = 0
            self.cnt[f"DG{i}"] = 0
        self.dma_rr = {"H": 0, "G": 0}
        self.hw = {e: {} for e in self.ENGINES}
        self.sems = {}
        self.psum_free = {}

    def _wait(self, eng, tok):
        if tok is None:
            return
        if isinstance(tok, list):
            for t in tok:
                self._wait(eng, t)
            return
        sem, val = tok
        if val <= self.hw[eng].get(sem, 0):
            return
        self.hw[eng][sem] = val
        self.items[eng].append(
            lambda e, sem=sem, val=val: e.wait_ge(self.sems[sem], val))

    def op(self, eng, fn, waits=(), tag=True):
        for t in waits:
            self._wait(eng, t)
        if tag:
            key = self.SEMKEY[eng]
            self.cnt[key] += 1
            v = self.cnt[key]

            def go(e, fn=fn, key=key):
                fn(e).then_inc(self.sems[key], 1)
            self.items[eng].append(go)
            return (key, v)
        self.items[eng].append(lambda e, fn=fn: fn(e))
        return None

    def _dma_sem(self, q):
        cls = "G" if q == "gpsimd" else "H"
        i = self.dma_rr[cls]
        self.dma_rr[cls] = (i + 1) % self.NDMASEM
        key = f"D{cls}{i}"
        prev = self.cnt[key]
        if prev:
            self._wait(q, (key, prev))
        self.cnt[key] += 16
        return key, self.cnt[key]

    def dma(self, q, out, in_, waits=(), **kw):
        for t in waits:
            self._wait(q, t)
        key, v = self._dma_sem(q)
        self.items[q].append(
            lambda e, out=out, in_=in_, kw=kw, key=key:
                e.dma_start(out=out, in_=in_, **kw).then_inc(self.sems[key], 16))
        return (key, v)

    def dmat(self, q, out, in_, waits=()):
        for t in waits:
            self._wait(q, t)
        key, v = self._dma_sem(q)
        self.items[q].append(
            lambda e, out=out, in_=in_, key=key:
                e.dma_start_transpose(out=out, in_=in_).then_inc(self.sems[key], 16))
        return (key, v)

    def cc(self, kind, aluop, in_ap, out_ap, waits=()):
        for t in waits:
            self._wait("gpsimd", t)
        key = "C"
        prev = self.cnt[key]
        if prev:
            self._wait("gpsimd", (key, prev))
        self.cnt[key] += 1
        v = self.cnt[key]
        self.items["gpsimd"].append(
            lambda e, kind=kind, aluop=aluop, in_ap=in_ap, out_ap=out_ap:
                e.collective_compute(kind, aluop, ins=[in_ap.opt()], outs=[out_ap.opt()],
                                     replica_groups=RG).then_inc(self.sems["C"], 1))
        return ("C", v)

    def tok(self, key):
        return (key, self.cnt[key])

    def finalize(self, nc, block):
        for eng in self.ENGINES:
            items = list(self.items[eng])
            deco = getattr(block, eng)

            def make(items):
                def body(e):
                    for fn in items:
                        fn(e)
                return body
            deco(make(items))


class Arena:
    def __init__(self, ap_f32, nbytes_per_part):
        self.ap = ap_f32
        self.nbytes = nbytes_per_part
        self.off = 0
        self.marks = []

    def carve(self, dtype, shape):
        assert shape[0] <= 128
        esz = {F32: 4, BF16: 2, F16: 2, I32: 4, U16: 2}[dtype]
        n = 1
        for sdim in shape[1:]:
            n *= sdim
        start = (self.off + 31) // 32 * 32
        self.off = start + n * esz
        assert self.off <= self.nbytes, f"arena overflow {self.off} > {self.nbytes}"
        w0 = start // 4
        view = self.ap[0:shape[0], w0:w0 + (n * esz + 3) // 4]
        if dtype != F32:
            view = view.bitcast(dtype)
        view = view[:, 0:n]
        if len(shape) > 2:
            pat = ("p (" + " ".join(f"d{i}" for i in range(1, len(shape))) + ") -> p "
                   + " ".join(f"d{i}" for i in range(1, len(shape))))
            kw = {f"d{i}": shape[i] for i in range(1, len(shape))}
            view = view.rearrange(pat, **kw)
        return view

    def push(self):
        self.marks.append(self.off)

    def pop(self):
        self.off = self.marks.pop()


ATTN_CFG = {
    0: dict(kv_per_head=True, qkv_bias=True, alibi=False, rope=False, bo=True),
    1: dict(kv_per_head=True, qkv_bias=True, alibi=True, rope=False, bo=True),
    2: dict(kv_per_head=False, qkv_bias=False, alibi=False, rope=False, bo=True),
    3: dict(kv_per_head=False, qkv_bias=False, alibi=False, rope=True, bo=False),
}


def emit_attn_block(p, s, li, stream_in_ap, stream_out_ap, filler=None):
    cfg = ATTN_CFG[li]
    A = s["arena"]
    A.push()
    srd = s.get("stream_ready") or []
    if not isinstance(srd, list):
        srd = [srd]
    t_bar = [p.tok("T"), p.tok("V"), p.tok("S")] + srd

    kvp = cfg["kv_per_head"]
    scr16 = A.carve(BF16, [128, T])
    PT = [A.carve(BF16, [128, 512]) for _ in range(2)]
    negm = A.carve(BF16, [1, T])
    s["negm_ap"] = negm
    mrow = A.carve(F32, [1, 512])
    rrow = A.carve(F32, [1, 512])
    nkcol = A.carve(F32, [1, 16])
    ccol = A.carve(F32, [128, 24])
    yt_sb = A.carve(BF16, [128, 512])
    rb_sb = A.carve(F32, [128, 512])
    if cfg["qkv_bias"]:
        bqs = A.carve(BF16, [1, 512]); bks = A.carve(BF16, [1, 512]); bvs = A.carve(BF16, [1, 512])
    if cfg["alibi"]:
        adneg = A.carve(F32, [1, T])
    if not cfg["kv_per_head"]:
        kt_hold = A.carve(BF16, [128, T])
        va_hold = A.carve(BF16, [128, T])
    if cfg["alibi"]:
        s["alibiT_sb"] = A.carve(F32, [128, 128])
        p._wait("scalar", p.dma("gpsimd", s["alibiT_sb"], s["inp"][f"L{li}_alibiT"],
                                waits=t_bar))
    if cfg["rope"]:
        s["ropeA_sb"] = A.carve(BF16, [128, T])
        s["ropeB_sb"] = A.carve(BF16, [128, T])
        ta_ = p.dma("gpsimd", s["ropeA_sb"], s["inp"][f"L{li}_ropeA"], waits=t_bar)
        s["rope_ready"] = p.dma("gpsimd", s["ropeB_sb"], s["inp"][f"L{li}_ropeB"])
        p._wait("vector", ta_)

    PS = s["PS"]
    PSB = s["PSB"]
    onesb_row = s["onesb_row"]
    onesb_r128 = s["onesb_r128"]
    onesb_col = s["onesb_col"]
    onesf_row = s["onesf_row"]
    identb = s["identb"]
    masksT = s["masksT"]
    PF = p.psum_free

    dram = s["dram"]
    ag_in, xg, xgt, ytd, rs_in, rs_out = (dram["ag_in"], dram["xg"], dram["xgt"],
                                          dram["ytd"], dram["rs_in"], dram["rs_out"])
    inp = s["inp"]
    wq, wk, wv, wo = inp[f"L{li}_wq"], inp[f"L{li}_wk"], inp[f"L{li}_wv"], inp[f"L{li}_wo"]

    t_ag_in = p.dma("gpsimd", ag_in, stream_in_ap, waits=srd + t_bar)
    t_ag = p.cc("AllGather", ALU.bypass, ag_in, xg, waits=[t_ag_in])

    t_wready = list(t_bar)
    if cfg["qkv_bias"]:
        t_wready.append(p.dma("gpsimd", bqs, inp[f"L{li}_bq_row"]))
        t_wready.append(p.dma("gpsimd", bks, inp[f"L{li}_bk_row"]))
        t_wready.append(p.dma("gpsimd", bvs, inp[f"L{li}_bv_row"]))

    A.push()
    trb = [A.carve(BF16, [128, T]) for _ in range(2)]
    last_store = [None, None]
    t_xgt_all = []
    for c in range(16):
        par = c % 2
        tt = p.dmat("sync", trb[par], xg[:, c * 128:(c + 1) * 128],
                    waits=t_bar + [t_ag, last_store[par]])
        tst = p.dma("sync", xgt[c * 128:(c + 1) * 128, :], trb[par], waits=[tt])
        last_store[par] = tst
        t_xgt_all.append(tst)
    A.pop()

    yt_tok = None
    sq = scr16[:, 0:T]
    t_scr_free = None
    t_negm_rd = None
    nk_done = {}
    for grp in range(2):
        heads = [2 * grp, 2 * grp + 1]
        do_kv = kvp or grp == 0
        nkv_g = 2 if kvp else (1 if grp == 0 else 0)
        A.push()
        QT = A.carve(BF16, [128, 2 * T])
        if kvp:
            KT = A.carve(BF16, [128, 2 * T]); VA = A.carve(BF16, [128, 2 * T])
        else:
            KT, VA = kt_hold, va_hold

        t_gbar = p.tok("T")
        t_gbarS = p.tok("S")
        t_gbarV = p.tok("V")
        A.push()
        xq = [A.carve(BF16, [128, 16, 512]) for _ in range(2)]
        ncolw = 256 + 128 * nkv_g * 2
        WQKV = A.carve(BF16, [128, 16, max(ncolw, 1)])
        t_w = []
        t_w.append(p.dma("gpsimd", WQKV[:, :, 0:256],
                         wq[:, 256 * grp:256 * (grp + 1)].rearrange("(c p) h -> p c h", p=128),
                         waits=[t_gbar, t_gbarS, t_gbarV]))
        if nkv_g:
            kc0 = 128 * 2 * grp if kvp else 0
            t_w.append(p.dma("gpsimd", WQKV[:, :, 256:256 + 128 * nkv_g],
                             wk[:, kc0:kc0 + 128 * nkv_g].rearrange("(c p) h -> p c h", p=128)))
            t_w.append(p.dma("gpsimd", WQKV[:, :, 256 + 128 * nkv_g:ncolw],
                             wv[:, kc0:kc0 + 128 * nkv_g].rearrange("(c p) h -> p c h", p=128)))
        t_w += t_wready
        vstage = scr16[:, 0:512]
        vstage_free = t_scr_free
        xq_tok = [t_gbar, t_gbar]
        t_proj = None
        for qc in range(NQC):
            par = qc % 2
            t_xq = p.dma("sync", xq[par][:, :, :],
                         xgt.rearrange("(c p) q -> p c q", p=128)[:, :, qc * 512:(qc + 1) * 512],
                         waits=t_xgt_all + [xq_tok[par]])
            outs = [("q", hi, hi * 128) for hi in range(2)]
            if nkv_g:
                outs += [("k", hi, 256 + hi * 128) for hi in range(nkv_g)]
                outs += [("v", hi, 256 + 128 * nkv_g + hi * 128) for hi in range(nkv_g)]
            last_out_tok = None
            for oi, (kind, hi, wcol) in enumerate(outs):
                pb = oi % 2
                hasb = cfg["qkv_bias"]
                for c in range(16):
                    w = t_w + [t_xq, PF.get(pb)] if c == 0 else []
                    tmm_ = p.op("tensor", lambda e, c=c, wcol=wcol, pb=pb, par=par, WQKV=WQKV, xq=xq:
                                e.matmul(PS[pb][:, :], WQKV[:, c, wcol:wcol + 128], xq[par][:, c, :],
                                         start=(c == 0), stop=(c == 15 and not hasb)),
                                waits=w, tag=(c == 15 and not hasb))
                if hasb:
                    brow = {"q": bqs, "k": bks, "v": bvs}[kind]
                    bcol = (2 * grp + hi) * 128
                    t_last = p.op("tensor", lambda e, brow=brow, bcol=bcol, pb=pb:
                                  e.matmul(PS[pb][:, :], brow[0:1, bcol:bcol + 128], onesb_row,
                                           start=False, stop=True))
                else:
                    t_last = tmm_
                if kind == "q":
                    dst = QT[:, hi * T + qc * 512: hi * T + qc * 512 + 512]
                    tev = p.op("scalar", lambda e, dst=dst, pb=pb:
                               e.activation(out=dst, in_=PS[pb][:, :], func=AF.Copy, bias=0.0,
                                            scale=SCALE), waits=[t_last, t_gbar])
                elif kind == "k":
                    dst = KT[:, hi * T + qc * 512: hi * T + qc * 512 + 512]
                    tev = p.op("scalar", lambda e, dst=dst, pb=pb:
                               e.activation(out=dst, in_=PS[pb][:, :], func=AF.Copy, bias=0.0,
                                            scale=1.0), waits=[t_last])
                else:
                    tev0 = p.op("scalar", lambda e, pb=pb, vstage=vstage:
                                e.activation(out=vstage, in_=PS[pb][:, :], func=AF.Copy, bias=0.0,
                                             scale=1.0), waits=[t_last, vstage_free])
                    tev = tev0
                    for j in range(4):
                        ks = qc * 4 + j
                        ttr = p.op("tensor", lambda e, j=j, vstage=vstage:
                                   e.transpose(PSB[7][:, 0:128], vstage[:, j * 128:(j + 1) * 128],
                                               identb), waits=[tev0, PF.get(7)])
                        tev = p.op("vector", lambda e, hi=hi, ks=ks, VA=VA:
                                   e.tensor_copy(VA[:, hi * T + ks * 128: hi * T + ks * 128 + 128],
                                                 PSB[7][:, 0:128]), waits=[ttr])
                        PF[7] = tev
                    vstage_free = tev
                PF[pb] = tev
                last_out_tok = tev
            xq_tok[par] = last_out_tok
            t_proj = last_out_tok
            if filler is not None:
                filler()
        A.pop()
        t_scr_free = vstage_free

        if s.get("dbg") is not None and grp == 0:
            nk_ = KT.shape[1]
            p.dma("sync", s["dbg"]["dbg_qt"], QT, waits=[t_proj])
            p.dma("sync", s["dbg"]["dbg_kt"][0:128, 0:nk_], KT, waits=[t_proj])
            p.dma("sync", s["dbg"]["dbg_va"][0:128, 0:nk_], VA, waits=[t_proj])
        if cfg["rope"]:
            ropeA = s["ropeA_sb"]; ropeB = s["ropeB_sb"]
            swp = scr16[:, 0:T]
            tprev = t_proj
            rop = [(QT, 0), (QT, 1)] + ([(KT, 0)] if grp == 0 else [])
            for (buf, hh) in rop:
                sl = buf[:, hh * T:(hh + 1) * T]
                t1 = p.dma("sync", swp[0:64, :], sl[64:128, :], waits=[t_proj, tprev, t_scr_free])
                t2 = p.dma("sync", swp[64:128, :], sl[0:64, :], waits=[])
                t3 = p.op("vector", lambda e, sl=sl: e.tensor_tensor(sl, sl, ropeA, op=ALU.mult),
                          waits=[t1, t2, s.get("rope_ready")])
                t4 = p.op("vector", lambda e: e.tensor_tensor(swp, swp, ropeB, op=ALU.mult),
                          waits=[t3])
                tprev = p.op("vector", lambda e, sl=sl: e.tensor_tensor(sl, sl, swp, op=ALU.add),
                             waits=[t4])
            t_proj = tprev
            t_scr_free = tprev

        for hi in range(nkv_g):
            hg = 2 * grp + hi if kvp else 0
            tq = p.op("scalar", lambda e, hi=hi, KT=KT: e.activation(
                out=sq, in_=KT[:, hi * T:(hi + 1) * T], func=AF.Square, bias=0.0, scale=1.0),
                waits=[t_proj, t_scr_free])
            tred = []
            prev_consumer = PF.get(6)
            for j in range(8):
                tmm = p.op("tensor", lambda e, j=j: e.matmul(
                    PS[6][0:1, :], onesb_col, sq[:, j * 512:(j + 1) * 512], start=True, stop=True),
                    waits=[tq, prev_consumer])
                tr8 = p.op("vector", lambda e, j=j: e.reduce_max(
                    nkcol[0:1, 8 + j:9 + j], PS[6][0:1, :], axis=AX.X), waits=[tmm])
                prev_consumer = tr8
                tred.append(tr8)
            PF[6] = prev_consumer
            tmx = p.op("vector", lambda e, hg=hg: e.reduce_max(
                nkcol[0:1, hg:hg + 1], nkcol[0:1, 8:16], axis=AX.X), waits=tred)
            tsq2 = p.op("scalar", lambda e, hg=hg: e.activation(
                out=nkcol[0:1, hg:hg + 1], in_=nkcol[0:1, hg:hg + 1], func=AF.Sqrt, bias=0.0,
                scale=1.0), waits=[tmx])
            tsc = p.op("vector", lambda e, hg=hg: e.tensor_scalar(
                nkcol[0:1, hg:hg + 1], nkcol[0:1, hg:hg + 1], -1.0, None, op0=ALU.mult),
                waits=[tsq2])
            nk_done[hg] = tsc
            t_scr_free = tsc

        for hi in range(2):
            h = 2 * grp + hi
            hkl = hi if kvp else 0
            hkg = h if kvp else 0
            if cfg["alibi"]:
                t_ad = p.dma("gpsimd", adneg, inp[f"L{li}_alibi_dneg"][0:1, h * T:(h + 1) * T],
                             waits=[s.get("adneg_rd")])
            tq = p.op("scalar", lambda e, hi=hi, QT=QT: e.activation(
                out=sq, in_=QT[:, hi * T:(hi + 1) * T], func=AF.Square, bias=0.0, scale=1.0),
                waits=[t_proj, t_scr_free, nk_done[hkg]])
            t_m = t_negm_rd
            prev_consumer = PF.get(6)
            for j in range(8):
                tmm = p.op("tensor", lambda e, j=j: e.matmul(
                    PS[6][0:1, :], onesb_col, sq[:, j * 512:(j + 1) * 512], start=True, stop=True),
                    waits=[tq, prev_consumer])
                tsr = p.op("scalar", lambda e, j=j: e.activation(
                    out=mrow, in_=PS[6][0:1, :], func=AF.Sqrt, bias=0.0, scale=1.0),
                    waits=[tmm, t_m])
                prev_consumer = tsr
                if cfg["alibi"]:
                    tm1 = p.op("vector", lambda e, hkg=hkg: e.tensor_scalar(
                        mrow, mrow, nkcol[0:1, hkg:hkg + 1], None, op0=ALU.mult), waits=[tsr])
                    t_m = p.op("vector", lambda e, j=j: e.tensor_tensor(
                        negm[0:1, j * 512:(j + 1) * 512], mrow, adneg[0:1, j * 512:(j + 1) * 512],
                        op=ALU.add), waits=[tm1, t_ad])
                else:
                    t_m = p.op("vector", lambda e, j=j, hkg=hkg: e.tensor_scalar(
                        negm[0:1, j * 512:(j + 1) * 512], mrow, nkcol[0:1, hkg:hkg + 1], None,
                        op0=ALU.mult), waits=[tsr])
                prev_consumer = t_m
            PF[6] = t_m
            t_scr_free = t_m
            if cfg["alibi"]:
                s["adneg_rd"] = t_m

            for qc in range(NQC):
                n_ks = 4 * (qc + 1)
                exp_toks = {}
                t_av = None
                for ks in range(n_ks):
                    pb = ks % 2
                    w = [t_m, exp_toks.get(pb)]
                    if ks == 0:
                        w += [PF.get(0), PF.get(1)]
                    p.op("tensor", lambda e, ks=ks, qc=qc, hi=hi, pb=pb, hkl=hkl, KT=KT, QT=QT:
                         e.matmul(PS[pb][:, :], KT[:, hkl * T + ks * 128: hkl * T + ks * 128 + 128],
                                  QT[:, hi * T + qc * 512: hi * T + qc * 512 + 512],
                                  start=True, stop=False), waits=w, tag=False)
                    if qc * 4 <= ks:
                        idx = ks - qc * 4
                        p.op("tensor", lambda e, idx=idx, pb=pb:
                             e.matmul(PS[pb][:, :], identb, masksT[idx], start=False, stop=False),
                             tag=False)
                    t2 = p.op("tensor", lambda e, qc=qc, pb=pb:
                              e.matmul(PS[pb][:, :], onesb_r128,
                                       negm[0:1, qc * 512:(qc + 1) * 512], start=False, stop=True))
                    t_negm_rd = t2
                    if cfg["alibi"]:
                        texp = p.op("scalar", lambda e, pb=pb, ks=ks, h=h:
                                    e.activation(out=PT[pb][:, :], in_=PS[pb][:, :], func=AF.Exp,
                                                 bias=s["alibiT_sb"][:, h * 32 + ks: h * 32 + ks + 1],
                                                 scale=1.0), waits=[t2])
                    else:
                        texp = p.op("scalar", lambda e, pb=pb:
                                    e.activation(out=PT[pb][:, :], in_=PS[pb][:, :], func=AF.Exp,
                                                 bias=0.0, scale=1.0), waits=[t2])
                    w3 = [texp]
                    if ks == 0:
                        w3 += [PF.get(2), PF.get(3)]
                    p.op("tensor", lambda e, pb=pb, ks=ks, n_ks=n_ks:
                         e.matmul(PS[2][0:1, :], onesb_col, PT[pb][:, :],
                                  start=(ks == 0), stop=(ks == n_ks - 1)), waits=w3, tag=False)
                    t_av = p.op("tensor", lambda e, pb=pb, ks=ks, hkl=hkl, n_ks=n_ks, VA=VA:
                                e.matmul(PS[3][:, :], VA[:, hkl * T + ks * 128: hkl * T + ks * 128 + 128],
                                         PT[pb][:, :], start=(ks == 0), stop=(ks == n_ks - 1)))
                    if filler is not None and qc >= 4 and ks == n_ks // 2:
                        filler()
                    exp_toks[pb] = t_av
                PF[0] = exp_toks.get(0)
                PF[1] = exp_toks.get(1)
                t5 = p.op("vector", lambda e: e.reciprocal(rrow, PS[2][0:1, :]),
                          waits=[t_av, PF.get("rrow")])
                PF[2] = t5
                t6 = p.op("tensor", lambda e: e.matmul(PS[4][:, :], onesf_row, rrow,
                                                       start=True, stop=True),
                          waits=[t5, PF.get(4)])
                PF["rrow"] = t6
                t7 = p.op("vector", lambda e: e.tensor_copy(rb_sb, PS[4][:, :]), waits=[t6])
                PF[4] = t7
                t8 = p.op("vector", lambda e: e.tensor_tensor(yt_sb, PS[3][:, :], rb_sb,
                                                              op=ALU.mult), waits=[t7, yt_tok])
                PF[3] = t8
                yt_tok = p.dma("sync", ytd[h * 128:(h + 1) * 128, qc * 512:(qc + 1) * 512],
                               yt_sb, waits=[t8])
                if filler is not None:
                    filler()
        A.pop()
    t_yt_all = yt_tok

    t_pbT, t_pbV, t_pbS = p.tok("T"), p.tok("V"), p.tok("S")
    phase_barrier = [t_pbT, t_pbV, t_pbS]
    A.push()
    WO = A.carve(BF16, [128, 4, D])
    bo_b = A.carve(F32, [128, D])
    rs_sb = [A.carve(BF16, [128, D]) for _ in range(2)]
    z_sb2 = [A.carve(F32, [128, D]) for _ in range(2)]
    xc_sb2 = [A.carve(F32, [128, D]) for _ in range(2)]
    lnscr2 = [A.carve(BF16, [128, D]) for _ in range(2)]
    borow = A.carve(F32, [1, D])
    t_wo_l = p.dma("gpsimd", WO[:, :, :], wo.rearrange("(c p) d -> p c d", p=128),
                   waits=phase_barrier)
    if cfg["bo"]:
        t_bor = p.dma("gpsimd", borow, inp[f"L{li}_bo_row"], waits=phase_barrier)
        t_bob = p.op("vector", lambda e: e.tensor_scalar(borow, borow, 0.25, None, op0=ALU.mult),
                     waits=[t_bor])
        t_bo_ready = None
        for j in range(4):
            tmm = p.op("tensor", lambda e, j=j: e.matmul(
                PS[6][:, :], onesf_row, borow[0:1, j * 512:(j + 1) * 512], start=True, stop=True),
                waits=[t_bob, PF.get(6)])
            t_bo_ready = p.op("vector", lambda e, j=j: e.tensor_copy(
                bo_b[:, j * 512:(j + 1) * 512], PS[6][:, :]), waits=[tmm])
            PF[6] = t_bo_ready
    else:
        t_bo_ready = p.op("vector", lambda e: e.memset(bo_b[:, :], 0.0), waits=phase_barrier)

    yqv = scr16[:, 0:2048].rearrange("p (c q) -> p c q", c=4)
    rs_toks = []
    yq_tok = t_scr_free
    rs_store = [None, None]
    for qc in range(NQC):
        t1 = p.dma("sync", yqv,
                   ytd.rearrange("(c p) q -> p c q", p=128)[:, :, qc * 512:(qc + 1) * 512],
                   waits=[t_yt_all, yq_tok])
        qc_toks = []
        for qs in range(4):
            pb = 4 + (qs % 2)
            par = qs % 2
            dc_toks = []
            for dc in range(4):
                for c in range(4):
                    w = [t1, t_wo_l, PF.get(pb)] if c == 0 else []
                    p.op("tensor", lambda e, c=c, qs=qs, dc=dc, pb=pb:
                         e.matmul(PS[pb][:, :], yqv[:, c, qs * 128:(qs + 1) * 128],
                                  WO[:, c, dc * 512:(dc + 1) * 512], start=(c == 0), stop=(c == 3)),
                         waits=w, tag=(c == 3))
                tl = p.tok("T")
                tev = p.op("vector", lambda e, dc=dc, pb=pb, par=par:
                           e.tensor_tensor(rs_sb[par][:, dc * 512:(dc + 1) * 512], PS[pb][:, :],
                                           bo_b[:, dc * 512:(dc + 1) * 512], op=ALU.add),
                           waits=[tl, t_bo_ready, rs_store[par]])
                PF[pb] = tev
                dc_toks.append(tev)
            tst = p.dma("sync", rs_in[qc * 512 + qs * 128: qc * 512 + (qs + 1) * 128, :],
                        rs_sb[par], waits=dc_toks)
            rs_store[par] = tst
            qc_toks.append(tst)
        yq_tok = qc_toks[-1]
        rs_toks.extend(qc_toks)
        if filler is not None:
            filler()
    t_rs = p.cc("ReduceScatter", ALU.add, rs_in, rs_out, waits=rs_toks)

    t_out = emit_residual_ln(p, s, rs_out, stream_in_ap, stream_out_ap,
                             z_sb2, xc_sb2, lnscr2, ccol, waits=[t_rs])
    A.pop()
    A.pop()
    return t_out


def emit_residual_ln(p, s, add_dram, stream_in_ap, stream_out_ap, z_sb2, xc_sb2, lnscr2, ccol,
                     waits=(), lnscr_war=None):
    srd = s.get("stream_ready") or []
    if not isinstance(srd, list):
        srd = [srd]
    outs = []
    pdone = [[], []]
    St = [{}, {}]

    def ln_head(i):
        par = i % 2
        c = St[par]
        z, xc = z_sb2[par], xc_sb2[par]
        t1 = p.dma("gpsimd", z, add_dram[i * 128:(i + 1) * 128, :],
                   waits=list(waits) + pdone[par])
        t2 = p.dma("sync", xc, stream_in_ap[i * 128:(i + 1) * 128, :], waits=srd + pdone[par])
        c["t3"] = p.op("vector", lambda e, z=z, xc=xc: e.tensor_tensor(z, z, xc, op=ALU.add),
                       waits=[t1, t2] + pdone[par])

    def ln_body(i):
        par = i % 2
        c = St[par]
        z, xc, lnscr = z_sb2[par], xc_sb2[par], lnscr2[par]
        cs = 8 + par * 5
        c0, c1, c2, c3, c4 = (ccol[:, cs + k:cs + k + 1] for k in range(5))
        t4 = p.op("vector", lambda e, z=z, c0=c0: e.reduce_sum(c0, z, axis=AX.X),
                  waits=[c["t3"]])
        t5 = p.op("vector", lambda e, c0=c0, c1=c1: e.tensor_scalar(c1, c0, -1.0 / D, None,
                                                                    op0=ALU.mult), waits=[t4])
        t6 = p.op("vector", lambda e, z=z, xc=xc, c1=c1: e.tensor_scalar(xc, z, c1, None,
                                                                         op0=ALU.add),
                  waits=[t5])
        t7 = p.op("scalar", lambda e, xc=xc, lnscr=lnscr, c2=c2: e.activation(
            out=lnscr, in_=xc, func=AF.Square, bias=0.0, scale=1.0, accum_out=c2),
            waits=[t6] + pdone[par])
        t8 = p.op("scalar", lambda e, c2=c2, c3=c3: e.activation(
            out=c3, in_=c2, func=AF.Sqrt, bias=1e-5, scale=1.0 / D), waits=[t7])
        t9 = p.op("vector", lambda e, c3=c3, c4=c4: e.reciprocal(c4, c3), waits=[t8])
        t10 = p.op("vector", lambda e, z=z, xc=xc, c4=c4: e.tensor_scalar(z, xc, c4, None,
                                                                          op0=ALU.mult),
                   waits=[t9])
        t11 = p.dma("sync", stream_out_ap[i * 128:(i + 1) * 128, :], z, waits=[t10])
        pdone[par] = [t11]
        outs.append(t11)

    for pi in range(NTT // 2):
        i0, i1 = 2 * pi, 2 * pi + 1
        ln_head(i0); ln_head(i1)
        ln_body(i0); ln_body(i1)
    return outs


def emit_ln_tile(p, s, z_sb, xc_sb, lnscr, ccol, out_ap_tile, waits=(), cslot=8):
    c0, c1, c2, c3, c4 = (ccol[:, cslot + k:cslot + k + 1] for k in range(5))
    t4 = p.op("vector", lambda e: e.reduce_sum(c0, z_sb, axis=AX.X), waits=list(waits))
    t5 = p.op("vector", lambda e: e.tensor_scalar(c1, c0, -1.0 / D, None, op0=ALU.mult), waits=[t4])
    t6 = p.op("vector", lambda e: e.tensor_scalar(xc_sb, z_sb, c1, None, op0=ALU.add), waits=[t5])
    t7 = p.op("scalar", lambda e: e.activation(out=lnscr, in_=xc_sb, func=AF.Square,
                                               bias=0.0, scale=1.0, accum_out=c2), waits=[t6])
    t8 = p.op("scalar", lambda e: e.activation(out=c3, in_=c2, func=AF.Sqrt,
                                               bias=1e-5, scale=1.0 / D), waits=[t7])
    t9 = p.op("vector", lambda e: e.reciprocal(c4, c3), waits=[t8])
    t10 = p.op("vector", lambda e: e.tensor_scalar(z_sb, xc_sb, c4, None, op0=ALU.mult), waits=[t9])
    return p.dma("sync", out_ap_tile, z_sb, waits=[t10])


def emit_xt_build(p, s, stream_in_ap, xT, xbf, waits=()):
    PSB = s["PSB"]; identb = s["identb"]; PF = p.psum_free
    t_last = None
    tl_prev = None
    for i in range(NTT):
        tl = p.dma("gpsimd", xbf, stream_in_ap[i * 128:(i + 1) * 128, :],
                   waits=list(waits) + [t_last])
        for c in range(16):
            ttr = p.op("tensor", lambda e, c=c, xbf=xbf:
                       e.transpose(PSB[7][:, 0:128], xbf[:, c * 128:(c + 1) * 128], identb),
                       waits=[tl, PF.get(7)])
            t_last = p.op("vector", lambda e, c=c, i=i, xT=xT:
                          e.tensor_copy(xT[:, c, i * 128:(i + 1) * 128], PSB[7][:, 0:128]),
                          waits=[ttr])
            PF[7] = t_last
    return t_last


def emit_ffn0_block(p, s, stream_in_ap, stream_out_ap):
    A = s["arena"]
    A.push()
    inp = s["inp"]
    PS = s["PS"]; PSB = s["PSB"]; PF = p.psum_free
    onesf_row = s["onesf_row"]
    srd = s.get("stream_ready") or []
    if not isinstance(srd, list):
        srd = [srd]
    t_bar = [p.tok("T"), p.tok("V"), p.tok("S")] + srd
    xT = A.carve(BF16, [128, 16, TC])
    xbf = A.carve(BF16, [128, D])
    swT = A.carve(BF16, [128, 16, 64])
    muT = A.carve(BF16, [128, 16, 8])
    updT = A.carve(BF16, [128, 16, 64])
    debT = A.carve(BF16, [128, 16, 8])
    dgnT = A.carve(BF16, [64, D])
    bias_sb = A.carve(BF16, [8, D])
    crow = A.carve(F32, [1, 64])
    c2row = A.carve(F32, [1, 64])
    mmrow = A.carve(F32, [1, 8])
    srow = A.carve(F32, [1, 8])
    cenrow = A.carve(F32, [1, 8])
    cb = A.carve(F32, [128, 64])
    c2b = A.carve(F32, [128, 64])
    slope_b = A.carve(F32, [128, 8])
    cent_b = A.carve(F32, [128, 8])
    mumu_b = A.carve(F32, [128, 8])
    z_sb = A.carve(F32, [128, D])
    xc_sb = A.carve(F32, [128, D])
    lnscr = A.carve(BF16, [128, D])
    ccol = A.carve(F32, [128, 16])
    scr = A.carve(F32, [128, 64])
    scr2 = A.carve(F32, [128, 64])
    num = A.carve(F32, [128, 8])
    d2 = A.carve(F32, [128, 8])
    g = A.carve(F32, [128, 8])
    oh = A.carve(F32, [128, 8])
    ohb = A.carve(BF16, [128, 8])
    mod_sb = A.carve(BF16, [128, 64])
    modT = A.carve(BF16, [64, 128])
    ohT = A.carve(BF16, [8, 128])
    tmpd = A.carve(F32, [128, 512])

    t_xt = emit_xt_build(p, s, stream_in_ap, xT, xbf, waits=t_bar + [s.get("stream_ready")])

    tw = [p.dma("gpsimd", swT[:, :, :], inp["L0f_swT"].rearrange("(c p) s -> p c s", p=128),
                waits=t_bar),
          p.dma("gpsimd", muT[:, :, :], inp["L0f_muT"].rearrange("(c p) s -> p c s", p=128)),
          p.dma("gpsimd", updT[:, :, :], inp["L0f_updT"].rearrange("(c p) s -> p c s", p=128)),
          p.dma("gpsimd", debT[:, :, :], inp["L0f_debT"].rearrange("(c p) s -> p c s", p=128)),
          p.dma("gpsimd", dgnT, inp["L0f_dgnT"]),
          p.dma("gpsimd", bias_sb, inp["L0f_bias"])]
    t_sl = p.dma("gpsimd", srow, inp["L0f_slope"], waits=t_bar)
    t_ce = p.dma("gpsimd", cenrow, inp["L0f_center"], waits=t_bar)

    prev = PF.get(6)
    t_crow = t_c2row = t_mm = None
    for n in range(8):
        for c in range(16):
            w = tw + [prev] if c == 0 else []
            p.op("tensor", lambda e, n=n, c=c: e.matmul(
                PS[6][0:1, 0:8], muT[:, c, n:n + 1], swT[:, c, n * 8:(n + 1) * 8],
                start=(c == 0), stop=(c == 15)), waits=w, tag=(c == 15))
        tl = p.tok("T")
        t_crow = p.op("vector", lambda e, n=n: e.tensor_copy(crow[0:1, n * 8:(n + 1) * 8],
                                                             PS[6][0:1, 0:8]), waits=[tl])
        prev = t_crow
    for n in range(8):
        for c in range(16):
            w = [prev] if c == 0 else []
            p.op("tensor", lambda e, n=n, c=c: e.matmul(
                PS[6][0:1, 0:8], debT[:, c, n:n + 1], updT[:, c, n * 8:(n + 1) * 8],
                start=(c == 0), stop=(c == 15)), waits=w, tag=(c == 15))
        tl = p.tok("T")
        t_c2row = p.op("vector", lambda e, n=n: e.tensor_copy(c2row[0:1, n * 8:(n + 1) * 8],
                                                              PS[6][0:1, 0:8]), waits=[tl])
        prev = t_c2row
    for n in range(8):
        for c in range(16):
            w = [prev] if c == 0 else []
            p.op("tensor", lambda e, n=n, c=c: e.matmul(
                PS[6][0:1, n:n + 1], muT[:, c, n:n + 1], muT[:, c, n:n + 1],
                start=(c == 0), stop=(c == 15)), waits=w, tag=(c == 15))
        prev = p.tok("T")
    t_mm = p.op("vector", lambda e: e.tensor_copy(mmrow, PS[6][0:1, 0:8]), waits=[prev])
    PF[6] = t_mm

    bc = []
    specs = [(crow, cb, 64, t_crow), (c2row, c2b, 64, t_c2row), (mmrow, mumu_b, 8, t_mm),
             (srow, slope_b, 8, t_sl), (cenrow, cent_b, 8, t_ce)]
    prev5 = PF.get(5)
    for row, dst, wdt, tok in specs:
        tm = p.op("tensor", lambda e, row=row, wdt=wdt: e.matmul(
            PS[5][:, 0:wdt], onesf_row, row[0:1, 0:wdt], start=True, stop=True),
            waits=[tok, prev5])
        tv = p.op("vector", lambda e, dst=dst, wdt=wdt: e.tensor_copy(dst[:, 0:wdt],
                                                                      PS[5][:, 0:wdt]), waits=[tm])
        prev5 = tv
        bc.append(tv)
    PF[5] = prev5
    t_bc = bc

    outs = []
    z_sb2 = [z_sb, A.carve(F32, [128, D])]
    xc_sb2 = [xc_sb, A.carve(F32, [128, D])]
    lnscr2 = [lnscr, A.carve(BF16, [128, D])]
    scr_p = [scr, A.carve(F32, [128, 64])]
    scr2_p = [scr2, A.carve(F32, [128, 64])]
    num_p = [num, A.carve(F32, [128, 8])]
    d2_p = [d2, A.carve(F32, [128, 8])]
    g_p = [g, A.carve(F32, [128, 8])]
    oh_p = [oh, A.carve(F32, [128, 8])]
    ohb_p = [ohb, A.carve(BF16, [128, 8])]
    mod_p = [mod_sb, A.carve(BF16, [128, 64])]
    modT_p = [modT, A.carve(BF16, [64, 128])]
    ohT_p = [ohT, A.carve(BF16, [8, 128])]
    tmpd_p = [tmpd, A.carve(F32, [128, 512])]
    ccol2 = A.carve(F32, [128, 32])
    Fst = [{}, {}]
    pdone0 = [[], []]
    ln_tok0 = None

    def f0_head(i):
        par = i % 2
        c = Fst[par] = {}
        pb_base = 3 * par
        zb = z_sb2[par]
        tl = p.dma("sync", zb, stream_in_ap[i * 128:(i + 1) * 128, :],
                   waits=srd + pdone0[par])
        c["txx"] = p.op("scalar", lambda e, zb=zb, par=par: e.activation(
            out=lnscr2[par], in_=zb, func=AF.Square, bias=0.0, scale=1.0,
            accum_out=ccol2[:, par * 16:par * 16 + 1]), waits=[tl] + pdone0[par])
        mm_t = {}
        for k, (rhs_sel, wdt) in enumerate((("sw", 64), ("mu", 8), ("upd", 64))):
            pb = pb_base + k
            for cch in range(16):
                w = [t_xt, PF.get(pb)] + tw if cch == 0 else []
                rhs = {"sw": swT, "mu": muT, "upd": updT}[rhs_sel]
                p.op("tensor", lambda e, pb=pb, cch=cch, rhs=rhs, wdt=wdt, i=i: e.matmul(
                    PS[pb][:, 0:wdt], xT[:, cch, i * 128:(i + 1) * 128], rhs[:, cch, 0:wdt],
                    start=(cch == 0), stop=(cch == 15)), waits=w, tag=(cch == 15))
            mm_t[k] = p.tok("T")
        c["mm_t"] = mm_t

    def f0_gate(i):
        par = i % 2
        c = Fst[par]
        pb_base = 3 * par
        cc = ccol2[:, par * 16:(par + 1) * 16]
        sc, s2, nm, dd, gg, oo = (scr_p[par], scr2_p[par], num_p[par], d2_p[par],
                                  g_p[par], oh_p[par])
        mm_t = c["mm_t"]
        t1 = p.op("vector", lambda e, sc=sc, pb=pb_base: e.scalar_tensor_tensor(
            sc, PS[pb][:, 0:64], 1.0, cb, op0=ALU.mult, op1=ALU.subtract),
            waits=[mm_t[0], t_bc[0]] + pdone0[par])
        PF[pb_base] = t1
        t2 = p.op("scalar", lambda e, sc=sc, s2=s2: e.activation(out=s2, in_=sc,
                                                                 func=AF.Square,
                                                                 bias=0.0, scale=1.0),
                  waits=[t1] + pdone0[par])
        tn_prev = t2
        for n in range(8):
            tn_prev = p.op("vector", lambda e, n=n, nm=nm, s2=s2: e.reduce_sum(
                nm[:, n:n + 1], s2[:, n * 8:(n + 1) * 8], axis=AX.X), waits=[tn_prev])
        t3 = p.op("vector", lambda e, dd=dd, pb=pb_base: e.scalar_tensor_tensor(
            dd, PS[pb + 1][:, 0:8], -2.0, mumu_b, op0=ALU.mult, op1=ALU.add),
            waits=[mm_t[1], t_bc[2]] + pdone0[par])
        PF[pb_base + 1] = t3
        t4 = p.op("vector", lambda e, dd=dd, cc=cc: e.tensor_scalar(dd, dd, cc[:, 0:1], None,
                                                                    op0=ALU.add),
                  waits=[t3, c["txx"]])
        t5a = p.op("vector", lambda e, dd=dd: e.reciprocal(dd, dd), waits=[t4])
        t5 = p.op("vector", lambda e, gg=gg, nm=nm, dd=dd: e.tensor_tensor(gg, nm, dd,
                                                                           op=ALU.mult),
                  waits=[t5a, tn_prev])
        t6 = p.op("vector", lambda e, gg=gg: e.tensor_tensor(gg, gg, cent_b, op=ALU.subtract),
                  waits=[t5, t_bc[4]])
        t7 = p.op("vector", lambda e, gg=gg: e.tensor_tensor(gg, gg, slope_b, op=ALU.mult),
                  waits=[t6, t_bc[3]])
        t8 = p.op("scalar", lambda e, gg=gg: e.activation(out=gg, in_=gg, func=AF.Sigmoid,
                                                          bias=0.0, scale=1.0), waits=[t7])
        t9 = p.op("vector", lambda e, gg=gg, cc=cc: e.reduce_max(cc[:, 1:2], gg, axis=AX.X),
                  waits=[t8])
        t10 = p.op("vector", lambda e, cc=cc: e.memset(cc[:, 2:3], 1.0),
                   waits=[t9] + pdone0[par])
        tp = t10
        for n in range(8):
            ta = p.op("vector", lambda e, n=n, oo=oo, gg=gg, cc=cc: e.tensor_scalar(
                oo[:, n:n + 1], gg[:, n:n + 1], cc[:, 1:2], None, op0=ALU.is_ge), waits=[tp])
            tb = p.op("vector", lambda e, n=n, oo=oo, cc=cc: e.tensor_tensor(
                oo[:, n:n + 1], oo[:, n:n + 1], cc[:, 2:3], op=ALU.mult), waits=[ta])
            tp = p.op("vector", lambda e, n=n, oo=oo, cc=cc: e.tensor_tensor(
                cc[:, 2:3], cc[:, 2:3], oo[:, n:n + 1], op=ALU.subtract), waits=[tb])
        c["tp"] = tp
        c["t11"] = p.op("vector", lambda e, cc=cc: e.tensor_scalar(
            cc[:, 3:4], cc[:, 1:2], -1.0, 2.0, op0=ALU.mult, op1=ALU.add), waits=[t9])

    def f0_tail(i):
        nonlocal ln_tok0
        par = i % 2
        c = Fst[par]
        pb_base = 3 * par
        cc = ccol2[:, par * 16:(par + 1) * 16]
        sc, oo, ob = scr_p[par], oh_p[par], ohb_p[par]
        msb, mtp, otp, tpd = mod_p[par], modT_p[par], ohT_p[par], tmpd_p[par]
        zb = z_sb2[par]
        t12 = p.op("vector", lambda e, sc=sc, pb=pb_base: e.scalar_tensor_tensor(
            sc, PS[pb + 2][:, 0:64], 1.0, c2b, op0=ALU.mult, op1=ALU.subtract),
            waits=[c["mm_t"][2], t_bc[1], c["tp"]])
        PF[pb_base + 2] = t12
        tm_prev = [t12]
        for n in range(8):
            tmsk = p.op("vector", lambda e, n=n, msb=msb, sc=sc, oo=oo: e.tensor_scalar(
                msb[:, n * 8:(n + 1) * 8], sc[:, n * 8:(n + 1) * 8], oo[:, n:n + 1], None,
                op0=ALU.mult), waits=tm_prev)
            tm_prev = [tmsk]
        t13 = p.op("vector", lambda e, ob=ob, oo=oo: e.tensor_copy(ob, oo), waits=[c["tp"]])
        tt1 = p.op("tensor", lambda e, msb=msb: e.transpose(PSB[7][0:64, 0:128], msb,
                                                            s["identb"]),
                   waits=tm_prev + [PF.get(7)])
        tt1e = p.op("vector", lambda e, mtp=mtp: e.tensor_copy(mtp, PSB[7][0:64, 0:128]),
                    waits=[tt1])
        PF[7] = tt1e
        tt2 = p.op("tensor", lambda e, ob=ob: e.transpose(PSB[6][0:8, 0:128], ob, s["identb"]),
                   waits=[t13, PF.get(6)])
        tt2e = p.op("vector", lambda e, otp=otp: e.tensor_copy(otp, PSB[6][0:8, 0:128]),
                    waits=[tt2])
        PF[6] = tt2e
        tz = p.op("vector", lambda e, zb=zb, cc=cc: e.tensor_scalar(zb, zb, cc[:, 3:4], None,
                                                                    op0=ALU.mult),
                  waits=[c["t11"], c["txx"]])
        tlast = tz
        for dc in range(4):
            pb = 6 + (dc % 2)
            p.op("tensor", lambda e, dc=dc, pb=pb, mtp=mtp: e.matmul(
                PS[pb][:, :], mtp, dgnT[:, dc * 512:(dc + 1) * 512], start=True, stop=False),
                waits=[tt1e, PF.get(pb)], tag=False)
            tmm2 = p.op("tensor", lambda e, dc=dc, pb=pb, otp=otp: e.matmul(
                PS[pb][:, :], otp, bias_sb[:, dc * 512:(dc + 1) * 512], start=False, stop=True),
                waits=[tt2e])
            te1 = p.op("vector", lambda e, pb=pb, tpd=tpd, cc=cc: e.tensor_scalar(
                tpd, PS[pb][:, :], cc[:, 1:2], None, op0=ALU.mult),
                waits=[tmm2, tz, tlast])
            PF[pb] = te1
            tlast = p.op("vector", lambda e, dc=dc, zb=zb, tpd=tpd: e.tensor_tensor(
                zb[:, dc * 512:(dc + 1) * 512], zb[:, dc * 512:(dc + 1) * 512],
                tpd, op=ALU.add), waits=[te1])
        ln_tok0 = emit_ln_tile(p, s, zb, xc_sb2[par], lnscr2[par], ccol2,
                               stream_out_ap[i * 128:(i + 1) * 128, :],
                               waits=[tlast, ln_tok0], cslot=par * 16 + 8)
        outs.append(ln_tok0)
        pdone0[par] = [p.tok("T"), ln_tok0]

    for pi in range(NTT // 2):
        i0, i1 = 2 * pi, 2 * pi + 1
        f0_head(i0); f0_head(i1)
        f0_gate(i0); f0_gate(i1)
        f0_tail(i0); f0_tail(i1)
    A.pop()
    return outs

def make_consts(p, s):
    A = s["arena"]
    s["onesb_row"] = A.carve(BF16, [1, 512])
    s["onesb_r128"] = A.carve(BF16, [1, 128])
    s["onesb_col"] = A.carve(BF16, [128, 1])
    s["onesf_row"] = A.carve(F32, [1, 128])
    s["zrow_b"] = A.carve(BF16, [1, 16])
    toks = [
        p.op("gpsimd", lambda e: e.memset(s["onesb_row"], 1.0)),
        p.op("gpsimd", lambda e: e.memset(s["onesb_r128"], 1.0)),
        p.op("gpsimd", lambda e: e.memset(s["onesb_col"], 1.0)),
        p.op("gpsimd", lambda e: e.memset(s["onesf_row"], 1.0)),
        p.op("gpsimd", lambda e: e.memset(s["zrow_b"], 0.0)),
    ]
    for val in (1e-5, 1e-6, 1.1920929e-07):
        cap = A.carve(F32, [128, 1])
        toks.append(p.op("gpsimd", lambda e, cap=cap, val=val: e.memset(cap, val)))
        s.setdefault("_nc").const_aps.register(F32, val, cap)
    s["identb"] = A.carve(BF16, [128, 128])
    s["masksT"] = [A.carve(BF16, [128, 512]) for _ in range(4)]
    toks.append(p.dma("gpsimd", s["identb"], s["inp"]["identb"]))
    for i in range(4):
        toks.append(p.dma("gpsimd", s["masksT"][i], s["inp"]["masksT"][i]))
    for eng in ("tensor", "vector", "scalar", "sync"):
        for t in toks:
            p._wait(eng, t)


NW32 = 47000
import os as _os
DEBUG_ATTN = _os.environ.get("DEBUG_ATTN", "0") == "1"


def emit_ffn2_block(p, s, stream_in_ap, stream_out_ap):
    A = s["arena"]
    A.push()
    inp = s["inp"]
    PS = s["PS"]; PSB = s["PSB"]; PF = p.psum_free
    onesf_row = s["onesf_row"]; identb = s["identb"]
    logits_d = s["dram"]["logits"]
    EPS32 = 1.1920929e-07
    NIT = 22
    srd = s.get("stream_ready") or []
    if not isinstance(srd, list):
        srd = [srd]
    t_bar = [p.tok("T"), p.tok("V"), p.tok("S")] + srd
    nw_b = A.carve(F32, [128, D])
    rcols = A.carve(F32, [128, NTT])
    x_sb = A.carve(F32, [128, D])
    xc_sb = A.carve(F32, [128, D])
    lnscr = A.carve(BF16, [128, D])
    ccol = A.carve(F32, [128, 16])
    itcol = A.carve(F32, [128, 1])
    sccol = A.carve(F32, [128, 1])
    genes_sb = A.carve(BF16, [128, 32, 256])
    wd_sb = A.carve(BF16, [128, 2, D])
    wu_sb = A.carve(BF16, [128, 2, D])
    trow = A.carve(F32, [1, 2])

    t_ts = p.dma("gpsimd", trow[0:1, 0:1], inp["L2f_ts"], waits=t_bar)
    t_sc2 = p.dma("gpsimd", trow[0:1, 1:2], inp["L2f_scale"], waits=t_bar)
    t1 = p.op("vector", lambda e: e.tensor_scalar_max(trow[0:1, 0:1], trow[0:1, 0:1], 0.1),
              waits=[t_ts])
    t2 = p.op("vector", lambda e: e.reciprocal(trow[0:1, 0:1], trow[0:1, 0:1]), waits=[t1])
    prev5 = PF.get(5)
    t3 = p.op("tensor", lambda e: e.matmul(PS[5][:, 0:2], onesf_row, trow, start=True, stop=True),
              waits=[t2, t_sc2, prev5])
    t4 = p.op("vector", lambda e: e.tensor_copy(itcol, PS[5][:, 0:1]), waits=[t3])
    t5 = p.op("vector", lambda e: e.tensor_copy(sccol, PS[5][:, 1:2]), waits=[t3])
    PF[5] = t5
    t_genes = p.dma("gpsimd", genes_sb[:, :, :], inp["L2f_genes"].rearrange("(c p) v -> p c v", p=128),
                    waits=t_bar)
    t_wd = p.dma("gpsimd", wd_sb[:, :, :], inp["L2f_WdT"].rearrange("(c p) v -> p c v", p=128),
                 waits=t_bar)
    t_wu = p.dma("gpsimd", wu_sb[:, :, :], inp["L2f_WuT"].rearrange("(c p) v -> p c v", p=128),
                 waits=t_bar)

    A.push()
    xT = A.carve(BF16, [128, 16, TC])
    xbf = A.carve(BF16, [128, D])
    nwrow = A.carve(F32, [1, D])
    t_nw = p.dma("gpsimd", nwrow, inp["L2f_nw"], waits=t_bar)
    t_nwb = None
    for j in range(4):
        tm = p.op("tensor", lambda e, j=j: e.matmul(PS[5][:, :], onesf_row,
                                                    nwrow[0:1, j * 512:(j + 1) * 512],
                                                    start=True, stop=True), waits=[t_nw, PF.get(5)])
        t_nwb = p.op("vector", lambda e, j=j: e.tensor_copy(nw_b[:, j * 512:(j + 1) * 512],
                                                            PS[5][:, :]), waits=[tm])
        PF[5] = t_nwb
    tx_prev = None
    t_xt = None
    for i in range(NTT):
        tl = p.dma("sync", x_sb, stream_in_ap[i * 128:(i + 1) * 128, :],
                   waits=(s.get("stream_ready") or []) + [tx_prev])
        tq = p.op("scalar", lambda e: e.activation(out=lnscr, in_=x_sb, func=AF.Square,
                                                   bias=0.0, scale=1.0, accum_out=ccol[:, 0:1]),
                  waits=[tl, tx_prev] + t_bar)
        tsq = p.op("scalar", lambda e: e.activation(out=ccol[:, 1:2], in_=ccol[:, 0:1],
                                                    func=AF.Sqrt, bias=EPS32, scale=1.0 / D),
                   waits=[tq])
        trc = p.op("vector", lambda e, i=i: e.reciprocal(rcols[:, i:i + 1], ccol[:, 1:2]),
                   waits=[tsq])
        tn1 = p.op("vector", lambda e, i=i: e.tensor_scalar(x_sb, x_sb, rcols[:, i:i + 1], None,
                                                            op0=ALU.mult), waits=[trc])
        tn2 = p.op("vector", lambda e: e.tensor_tensor(xbf, x_sb, nw_b, op=ALU.mult),
                   waits=[tn1, t_nwb])
        tx_prev = tn2
        for c in range(16):
            ttr = p.op("tensor", lambda e, c=c: e.transpose(PSB[7][:, 0:128],
                                                            xbf[:, c * 128:(c + 1) * 128], identb),
                       waits=[tn2, PF.get(7)])
            t_xt = p.op("vector", lambda e, c=c, i=i: e.tensor_copy(xT[:, c, i * 128:(i + 1) * 128],
                                                                    PSB[7][:, 0:128]), waits=[ttr])
            PF[7] = t_xt
        tx_prev = t_xt

    A.push()
    wct = [A.carve(BF16, [128, 16, 512]) for _ in range(2)]
    lg_ev = A.carve(F32, [128, 512])
    wct_tok = [None, None]
    ev_tok = None
    for gc in range(8):
        par = gc % 2
        tw = p.dma("gpsimd", wct[par][:, :, :],
                   inp["L2f_WcT"].rearrange("(c p) g -> p c g", p=128)[:, :, gc * 512:(gc + 1) * 512],
                   waits=[wct_tok[par]] + t_bar)
        for i in range(NTT):
            pb = 2 + i % 2
            for c in range(16):
                w = [tw, t_xt, PF.get(pb)] if c == 0 else []
                p.op("tensor", lambda e, c=c, i=i, pb=pb, par=par: e.matmul(
                    PS[pb][:, :], xT[:, c, i * 128:(i + 1) * 128], wct[par][:, c, :],
                    start=(c == 0), stop=(c == 15)), waits=w, tag=(c == 15))
            tl = p.tok("T")
            tev = p.op("vector", lambda e, pb=pb: e.tensor_scalar(lg_ev, PS[pb][:, :],
                                                                  itcol[:, 0:1],
                                                                  None, op0=ALU.mult),
                       waits=[tl, t4, ev_tok])
            PF[pb] = tev
            ev_tok = p.dma("sync", logits_d[i * 128:(i + 1) * 128, gc * 512:(gc + 1) * 512],
                           lg_ev, waits=[tev])
            wct_tok[par] = ev_tok
    t_lg = ev_tok
    A.pop()
    A.pop()

    outs = []
    A.push()
    NPAR = 2
    lg_t = [A.carve(F32, [128, 4096]) for _ in range(NPAR)]
    s_bf1 = A.carve(BF16, [128, 4096])
    wm = [A.carve(BF16, [128, 4096]) for _ in range(NPAR)]
    wmT1 = A.carve(BF16, [128, 32, 128])
    expr1 = A.carve(F32, [128, 256])
    exprb1 = A.carve(BF16, [128, 256])
    exprT1 = A.carve(BF16, [128, 2, 128])
    dn_s = [A.carve(F32, [128, 512]) for _ in range(NPAR)]
    up_s = [A.carve(F32, [128, 512]) for _ in range(NPAR)]
    xn_s = [A.carve(F32, [128, 512]) for _ in range(NPAR)]
    x_sb3 = [A.carve(F32, [128, D]) for _ in range(NPAR)]
    zo2 = [A.carve(F32, [128, D]) for _ in range(NPAR)]
    pcol = [A.carve(F32, [128, 12]) for _ in range(NPAR)]
    parity_done = [[], []]
    ln_tok = None
    NIT2 = 20
    C = {}

    def f2_load(i):
        par = i % 2
        c = C[par] = {}
        lg = lg_t[par]; pc = pcol[par]
        pd = parity_done[par]
        c["lg"] = lg; c["pc"] = pc; c["pd"] = pd
        tl = p.dma("sync", lg, logits_d[i * 128:(i + 1) * 128, :], waits=[t_lg] + pd)
        tmx = p.op("vector", lambda e, lg=lg, pc=pc: e.reduce_max(pc[:, 1:2], lg, axis=AX.X),
                   waits=[tl] + pd)
        tmn = p.op("vector", lambda e, lg=lg, pc=pc: e.tensor_reduce(pc[:, 0:1], lg, axis=AX.X,
                                                                     op=ALU.min),
                   waits=[tl, tmx])
        tlo = p.op("vector", lambda e, pc=pc: e.tensor_scalar(pc[:, 0:1], pc[:, 0:1],
                                                              -1.0, None, op0=ALU.add),
                  waits=[tmn])
        c["tprev"] = p.op("vector", lambda e, pc=pc: e.scalar_tensor_tensor(
            pc[:, 9:10], pc[:, 0:1], -1.0, pc[:, 1:2], op0=ALU.mult, op1=ALU.add),
            waits=[tlo])
        c["it"] = 0

    def f2_iter(i):
        par = i % 2
        c = C[par]
        lg = c["lg"]; pc = c["pc"]; wmp = wm[par]
        lo = pc[:, 0:1]; w0 = pc[:, 9:10]
        bsel = pc[:, 3:4]; bneg = pc[:, 4:5]; cnt = pc[:, 5:6]; tt_ = pc[:, 6:7]
        sk = 2.0 ** (-c["it"] - 1)
        c["it"] += 1
        tprev = c["tprev"]
        tb = p.op("vector", lambda e, bneg=bneg, w0=w0, lo=lo, sk=sk: e.scalar_tensor_tensor(
            bneg, w0, -sk, lo, op0=ALU.mult, op1=ALU.subtract), waits=[tprev])
        tc = p.op("scalar", lambda e, wmp=wmp, lg=lg, bneg=bneg, cnt=cnt: e.activation(
            out=wmp, in_=lg, func=AF.Sign, bias=bneg, scale=1.0, accum_out=cnt),
            waits=[tb] + c["pd"])
        td = p.op("vector", lambda e, bsel=bsel, cnt=cnt: e.tensor_scalar(
            bsel, cnt, -4032.5, None, op0=ALU.is_ge), waits=[tc])
        tg_ = p.op("vector", lambda e, tt_=tt_, bsel=bsel, w0=w0: e.tensor_tensor(
            tt_, bsel, w0, op=ALU.mult), waits=[td])
        c["tprev"] = p.op("vector", lambda e, lo=lo, tt_=tt_, sk=sk: e.scalar_tensor_tensor(
            lo, tt_, sk, lo, op0=ALU.mult, op1=ALU.add), waits=[tg_])

    def f2_post(i):
        par = i % 2
        c = C[par]
        lg = c["lg"]; pc = c["pc"]; wmp = wm[par]; sb = s_bf1
        nmx = pc[:, 6:7]; zz = pc[:, 7:8]; rz = pc[:, 8:9]; lo = pc[:, 0:1]
        tprev = c["tprev"]
        t6 = p.op("vector", lambda e, nmx=nmx, lg=lg: e.reduce_max(nmx, lg, axis=AX.X),
                  waits=[tprev])
        t6b = p.op("vector", lambda e, nmx=nmx: e.tensor_scalar(nmx, nmx, -1.0, None,
                                                                op0=ALU.mult), waits=[t6])
        t7 = p.op("scalar", lambda e, wmp=wmp, lg=lg, nmx=nmx: e.activation(
            out=wmp, in_=lg, func=AF.Exp, bias=nmx, scale=1.0), waits=[t6b] + c["pd"])
        t8 = p.op("vector", lambda e, sb=sb, lg=lg, lo=lo: e.tensor_scalar(
            sb, lg, lo, None, op0=ALU.is_gt), waits=[tprev, t7, s.get("_f2_t9")])
        t9 = p.op("vector", lambda e, wmp=wmp, sb=sb: e.tensor_tensor(
            wmp, wmp, sb, op=ALU.mult), waits=[t8, t7] + c["pd"])
        s["_f2_t9"] = t9
        t10 = p.op("vector", lambda e, zz=zz, wmp=wmp: e.reduce_sum(zz, wmp, axis=AX.X),
                   waits=[t9])
        c["t11"] = p.op("vector", lambda e, rz=rz, zz=zz: e.reciprocal(rz, zz), waits=[t10])
        c["t9"] = t9

    def f2_heavy(i):
        nonlocal ln_tok
        par = i % 2
        c = C[par]
        pc = c["pc"]; wmp = wm[par]
        rz = pc[:, 8:9]
        xsb = x_sb3[par]; zop = zo2[par]
        dn = dn_s[par]; up = up_s[par]; xn = xn_s[par]
        twt = None
        for gc in range(32):
            ttr = p.op("tensor", lambda e, gc=gc, wmp=wmp: e.transpose(
                PSB[7][:, 0:128], wmp[:, gc * 128:(gc + 1) * 128], identb),
                waits=[c["t9"], PF.get(7)])
            twt = p.op("vector", lambda e, gc=gc: e.tensor_copy(
                wmT1[:, gc, :], PSB[7][:, 0:128]),
                waits=[ttr, s.get("_f2_wmT_rd")] if gc == 0 else [ttr])
            PF[7] = twt
        for gc in range(32):
            w = [twt, t_genes, PF.get(4)] if gc == 0 else []
            p.op("tensor", lambda e, gc=gc: e.matmul(
                PS[4][:, 0:256], wmT1[:, gc, :], genes_sb[:, gc, :], start=(gc == 0),
                stop=(gc == 31)), waits=w, tag=(gc == 31))
        tl2 = p.tok("T")
        s["_f2_wmT_rd"] = tl2
        t12 = p.op("vector", lambda e, rz=rz: e.tensor_scalar(
            expr1, PS[4][:, 0:256], rz, None, op0=ALU.mult),
            waits=[tl2, c["t11"], s.get("_f2_expr_rd")])
        PF[4] = t12
        t13 = p.op("vector", lambda e: e.tensor_copy(exprb1, expr1), waits=[t12])
        tet = None
        for cc2 in range(2):
            ttr = p.op("tensor", lambda e, cc2=cc2: e.transpose(
                PSB[6][:, 0:128], exprb1[:, cc2 * 128:(cc2 + 1) * 128], identb),
                waits=[t13, PF.get(6), s.get("_f2_expr_rd")])
            tet = p.op("vector", lambda e, cc2=cc2: e.tensor_copy(
                exprT1[:, cc2, :], PSB[6][:, 0:128]), waits=[ttr])
            PF[6] = tet
        tlx = p.dma("sync", xsb, stream_in_ap[i * 128:(i + 1) * 128, :], waits=c["pd"])
        tout = None
        for dc in range(4):
            for cc2 in range(2):
                w = [tet, t_wd, PF.get(0)] if cc2 == 0 else []
                p.op("tensor", lambda e, dc=dc, cc2=cc2: e.matmul(
                    PS[0][:, :], exprT1[:, cc2, :], wd_sb[:, cc2, dc * 512:(dc + 1) * 512],
                    start=(cc2 == 0), stop=(cc2 == 1)), waits=w, tag=(cc2 == 1))
            tld = p.tok("T")
            for cc2 in range(2):
                w = [tet, t_wu, PF.get(1)] if cc2 == 0 else []
                p.op("tensor", lambda e, dc=dc, cc2=cc2: e.matmul(
                    PS[1][:, :], exprT1[:, cc2, :], wu_sb[:, cc2, dc * 512:(dc + 1) * 512],
                    start=(cc2 == 0), stop=(cc2 == 1)), waits=w, tag=(cc2 == 1))
            tlu = p.tok("T")
            s["_f2_expr_rd"] = tlu
            ta1 = p.op("scalar", lambda e, dn=dn: e.activation(out=dn, in_=PS[0][:, :],
                                                               func=AF.Tanh, bias=0.0,
                                                               scale=1.0),
                       waits=[tld, tout] + c["pd"])
            PF[0] = ta1
            ta2 = p.op("vector", lambda e, up=up: e.tensor_copy(up, PS[1][:, :]),
                       waits=[tlu, tout] + c["pd"])
            PF[1] = ta2
            ta3 = p.op("vector", lambda e, dc=dc, i=i, xn=xn, xsb=xsb: e.tensor_scalar(
                xn, xsb[:, dc * 512:(dc + 1) * 512], rcols[:, i:i + 1], None, op0=ALU.mult),
                waits=[tlx, ta2])
            ta4 = p.op("vector", lambda e, dc=dc, xn=xn: e.tensor_tensor(
                xn, xn, nw_b[:, dc * 512:(dc + 1) * 512], op=ALU.mult), waits=[ta3])
            ta5 = p.op("vector", lambda e, dn=dn, up=up: e.tensor_tensor(
                dn, dn, up, op=ALU.mult), waits=[ta1, ta2, ta4])
            ta6 = p.op("vector", lambda e, dn=dn, xn=xn: e.tensor_tensor(
                dn, dn, xn, op=ALU.mult), waits=[ta5])
            ta7 = p.op("vector", lambda e, dn=dn: e.tensor_scalar(
                dn, dn, sccol[:, 0:1], None, op0=ALU.mult), waits=[ta6, t5])
            tout = p.op("vector", lambda e, dc=dc, zop=zop, xsb=xsb, dn=dn: e.tensor_tensor(
                zop[:, dc * 512:(dc + 1) * 512], xsb[:, dc * 512:(dc + 1) * 512], dn,
                op=ALU.add), waits=[ta7])
        ln_tok = emit_ln_tile(p, s, zop, xsb, lnscr, ccol,
                              stream_out_ap[i * 128:(i + 1) * 128, :],
                              waits=[tout, ln_tok], cslot=11)
        outs.append(ln_tok)
        parity_done[par] = [p.tok("T"), tout, ln_tok]

    for pi in range(4):
        i0, i1 = 2 * pi, 2 * pi + 1
        f2_load(i0); f2_load(i1)
        for it in range(NIT2):
            f2_iter(i0); f2_iter(i1)
        f2_post(i0); f2_post(i1)
        f2_heavy(i0); f2_heavy(i1)
    A.pop()
    A.pop()
    return outs


class F3Stepper:

    def __init__(self, p, s, which, other=None, per_site=1):
        self.p = p
        self.s = s
        self.which = which
        self.per_site = per_site
        A = s["arena"]
        wmax = 2048
        npar = 1 if which < 2 else 2
        self.npar = npar
        self.pk = [A.carve(I32, [128, 256]) for _ in range(npar)]
        self.nib9 = A.carve(I32, [128, wmax])
        self.tmpi = A.carve(I32, [128, wmax])
        self.u16v = [A.carve(U16, [128, wmax]) for _ in range(npar)]
        self.scl = [A.carve(F16, [128, 44]) for _ in range(npar)]
        self.sclf = [A.carve(F32, [128, 44]) for _ in range(npar)]
        self.wval = [A.carve(BF16, [128, wmax]) for _ in range(npar)]
        self.trb = A.carve(BF16, [128, 2816]) if which >= 2 else None
        self.bar = [p.tok("T"), p.tok("V"), p.tok("S")] + (s.get("stream_ready") or [])
        self.pk_tok = [None] * npar
        self.act_done = [None] * npar
        self.st_par = [None] * npar
        self._ph = [None] * npar
        self.pending = []
        self.tr_tok = None
        if other is not None:
            self.gu_toks = other.gu_toks
            self.dn_toks = other.dn_toks
            self.wgu_toks = other.wgu_toks
            self.wdn_toks = other.wdn_toks
        else:
            self.gu_toks = []
            self.dn_toks = []
            self.wgu_toks = []
            self.wdn_toks = []
        inp = s["inp"]
        dram = s["dram"]
        gu_list = [("dq", inp["L3f_gup"], inp["L3f_gusT"], dram["wgu"],
                    oc, 0, 256, D, 16, self.gu_toks) for oc in range(88)]
        trg_list = [("trg", hc, hq) for hc in range(16) for hq in range(4)]
        dn_list = [("dq", inp["L3f_dnp"], inp["L3f_dnsT"], dram["wdn"],
                    oc, hf, 176, 1408, 11, self.dn_toks)
                   for oc in range(16) for hf in range(4)]
        trd_list = [("trd", ic, 0) for ic in range(44)]
        if which == 0:
            self.units = gu_list[0:44]
        elif which == 1:
            self.units = gu_list[44:88]
        elif which == 2:
            self.units = []
            for i in range(32):
                self.units.append(trg_list[2 * i])
                self.units.append(trg_list[2 * i + 1])
                self.units.append(dn_list[i])
        else:
            self.units = list(dn_list[32:64])
            self.units.extend(trd_list)
        self.idx = 0
        self.pari = 0

    def _dq_dve(self, packed_ap, scales_ap, out_d, oc, hf, ncols, nh, ngrp, toks):
        p, s = self.p, self.s
        par = self.pari
        self.pari = (self.pari + 1) % self.npar
        pk, nib9, tmpi = self.pk, self.nib9, self.tmpi
        u16v, scl, sclf, wval = self.u16v, self.scl, self.sclf, self.wval
        tpk = p.dma("sync", pk[par][:, 0:ncols],
                    packed_ap[oc * 128:(oc + 1) * 128, hf * ncols:(hf + 1) * ncols],
                    waits=[self.pk_tok[par]] if self.pk_tok[par] is not None else self.bar)
        tsc = p.dma("sync", scl[par][:, 0:ngrp],
                    scales_ap[oc * 128:(oc + 1) * 128, hf * ngrp:(hf + 1) * ngrp],
                    waits=[] if self.idx > 1 else self.bar)
        tf = p.op("vector", lambda e, par=par, ngrp=ngrp: e.tensor_copy(
            sclf[par][:, 0:ngrp], scl[par][:, 0:ngrp]), waits=[tsc, self.act_done[par]])
        tf2 = p.op("vector", lambda e, par=par, ngrp=ngrp: e.tensor_scalar(
            sclf[par][:, 0:ngrp], sclf[par][:, 0:ngrp], 16384.0, None, op0=ALU.mult),
            waits=[tf])
        tun = None
        for j in range(8):
            sh = 4 * j - 9
            op0 = ALU.logical_shift_right if sh >= 0 else ALU.logical_shift_left
            tun = p.op("vector", lambda e, sh=abs(sh), op0=op0, j=j, par=par, ncols=ncols, nh=nh:
                       e.tensor_scalar(nib9[:, j:nh:8], pk[par][:, 0:ncols], sh, 0x1E00,
                                       op0=op0, op1=ALU.bitwise_and),
                       waits=[tpk] if j == 0 else [], tag=(j == 7))
        t1 = p.op("vector", lambda e, nh=nh: e.tensor_scalar(
            tmpi[:, 0:nh], nib9[:, 0:nh], 0x1000, 3, op0=ALU.bitwise_and,
            op1=ALU.logical_shift_left), waits=[tun])
        t1b = p.op("vector", lambda e, nh=nh: e.tensor_scalar(
            nib9[:, 0:nh], nib9[:, 0:nh], 0xE00, None, op0=ALU.bitwise_and), waits=[t1])
        t2a = p.op("vector", lambda e, nh=nh: e.tensor_tensor(
            tmpi[:, 0:nh], tmpi[:, 0:nh], nib9[:, 0:nh], op=ALU.bitwise_or), waits=[t1b])
        t2 = p.op("vector", lambda e, par=par, nh=nh: e.tensor_copy(
            u16v[par][:, 0:nh], tmpi[:, 0:nh]), waits=[t2a, self.act_done[par]])
        self.pk_tok[par] = tun
        self._ph[par] = (t2, tf2)
        return par

    def _dq_act(self, out_d, oc, hf, nh, ngrp, toks, par):
        p = self.p
        wval, sclf, u16v = self.wval, self.sclf, self.u16v
        t2, tf2 = self._ph[par]
        tg = None
        for gg in range(ngrp):
            tg = p.op("scalar", lambda e, gg=gg, par=par: e.activation(
                out=wval[par][:, gg * 128:(gg + 1) * 128],
                in_=u16v[par][:, gg * 128:(gg + 1) * 128].bitcast(F16),
                func=AF.Copy, bias=0.0, scale=sclf[par][:, gg:gg + 1]),
                waits=[t2, tf2, self.st_par[par]] if gg == 0 else [], tag=(gg == ngrp - 1))
        self.act_done[par] = tg
        st = p.dma("sync", out_d[oc * 128:(oc + 1) * 128, hf * nh:(hf + 1) * nh],
                   wval[par][:, 0:nh], waits=[tg])
        self.st_par[par] = st
        toks.append(st)

    def _tr(self, kind, hc, hf):
        p, s = self.p, self.s
        dram = s["dram"]
        if kind == "trg":
            src_d, dst_d, toks, deps = dram["wgu"], dram["wguT"], self.wgu_toks, self.gu_toks
            tt = p.dmat("scalar", self.trb[:, 0:2816],
                        src_d[hf * 2816:(hf + 1) * 2816, hc * 128:(hc + 1) * 128],
                        waits=deps + [self.tr_tok])
            ts_ = p.dma("scalar", dst_d[hc * 128:(hc + 1) * 128, hf * 2816:(hf + 1) * 2816],
                        self.trb[:, 0:2816], waits=[tt])
        else:
            src_d, dst_d, toks, deps = dram["wdn"], dram["wdnT"], self.wdn_toks, self.dn_toks
            tt = p.dmat("scalar", self.trb[:, 0:2048],
                        src_d[:, hc * 128:(hc + 1) * 128], waits=deps + [self.tr_tok])
            ts_ = p.dma("scalar", dst_d[hc * 128:(hc + 1) * 128, :],
                        self.trb[:, 0:2048], waits=[tt])
        self.tr_tok = ts_
        toks.append(ts_)

    def step(self, n=None):
        for _ in range(n if n is not None else self.per_site):
            if self.pending:
                self.pending.pop(0)()
                continue
            if self.idx >= len(self.units):
                return
            u = self.units[self.idx]
            self.idx += 1
            if u[0] == "dq":
                _, packed_ap, scales_ap, out_d, oc, hf, ncols, nh, ngrp, toks = u
                par = self._dq_dve(packed_ap, scales_ap, out_d, oc, hf, ncols, nh, ngrp, toks)
                self.pending.append(
                    lambda out_d=out_d, oc=oc, hf=hf, nh=nh, ngrp=ngrp, toks=toks, par=par:
                    self._dq_act(out_d, oc, hf, nh, ngrp, toks, par))
            else:
                self._tr(*u)
                if self.idx < len(self.units) and self.units[self.idx][0] in ("trg", "trd"):
                    u2 = self.units[self.idx]
                    self.idx += 1
                    self._tr(*u2)

    def finish(self):
        while self.pending or self.idx < len(self.units):
            self.step(1)
        return self.gu_toks + self.dn_toks + self.wgu_toks + self.wdn_toks


def emit_ffn3_block(p, s, stream_in_ap, stream_out_ap):
    A = s["arena"]
    A.push()
    inp = s["inp"]
    PS = s["PS"]; PSB = s["PSB"]; PF = p.psum_free
    identb = s["identb"]
    dram = s["dram"]
    wgu_d, wguT_d, wdn_d, wdnT_d = dram["wgu"], dram["wguT"], dram["wdn"], dram["wdnT"]
    IDIM = 5632
    srd = s.get("stream_ready") or []
    if not isinstance(srd, list):
        srd = [srd]
    t_bar = [p.tok("T"), p.tok("V"), p.tok("S")] + srd

    xT = A.carve(BF16, [128, 16, TC])
    xbf = A.carve(BF16, [128, D])
    t_xt = emit_xt_build(p, s, stream_in_ap, xT, xbf, waits=t_bar + [s.get("stream_ready")])

    if s.get("f3_pre") is not None:
        wgu_toks, wdn_toks = s["f3_pre"]
    else:
        A.push()
        pk = [A.carve(I32, [128, 704]) for _ in range(2)]
        nib9 = A.carve(I32, [128, IDIM])
        tmpi = A.carve(I32, [128, IDIM])
        u16v = [A.carve(U16, [128, IDIM]) for _ in range(2)]
        scl = [A.carve(F16, [128, 44]) for _ in range(2)]
        sclf = [A.carve(F32, [128, 44]) for _ in range(2)]
        wval = [A.carve(BF16, [128, IDIM]) for _ in range(2)]
        trb = [A.carve(BF16, [128, 11264]) for _ in range(2)]
        pk_tok = [None, None]
        act_done = [None, None]
        st_par = [None, None]
        st_tok = None

        def dequant(n_oc, ncols, nh, packed_ap, scales_ap, out_d):
            nonlocal st_tok
            ngrp = nh // 128
            for oc in range(n_oc):
                par = oc % 2
                tpk = p.dma("sync", pk[par][:, 0:ncols],
                            packed_ap[oc * 128:(oc + 1) * 128, :], waits=[pk_tok[par]] + t_bar)
                tsc = p.dma("sync", scl[par][:, 0:ngrp],
                            scales_ap[oc * 128:(oc + 1) * 128, :], waits=[])
                tf = p.op("vector", lambda e, par=par, ngrp=ngrp: e.tensor_copy(
                    sclf[par][:, 0:ngrp], scl[par][:, 0:ngrp]), waits=[tsc, act_done[par]])
                tf2 = p.op("vector", lambda e, par=par, ngrp=ngrp: e.tensor_scalar(
                    sclf[par][:, 0:ngrp], sclf[par][:, 0:ngrp], 16384.0, None, op0=ALU.mult),
                    waits=[tf])
                tun = None
                for j in range(8):
                    sh = 4 * j - 9
                    op0 = ALU.logical_shift_right if sh >= 0 else ALU.logical_shift_left
                    tun = p.op("vector", lambda e, sh=abs(sh), op0=op0, j=j, par=par, ncols=ncols, nh=nh:
                               e.tensor_scalar(nib9[:, j:nh:8], pk[par][:, 0:ncols], sh, 0x1E00,
                                               op0=op0, op1=ALU.bitwise_and),
                               waits=[tpk] if j == 0 else [], tag=(j == 7))
                t1 = p.op("vector", lambda e, nh=nh: e.tensor_scalar(
                    tmpi[:, 0:nh], nib9[:, 0:nh], 0x1000, 3, op0=ALU.bitwise_and,
                    op1=ALU.logical_shift_left), waits=[tun])
                t1b = p.op("vector", lambda e, nh=nh: e.tensor_scalar(
                    nib9[:, 0:nh], nib9[:, 0:nh], 0xE00, None, op0=ALU.bitwise_and), waits=[t1])
                t2a = p.op("vector", lambda e, nh=nh: e.tensor_tensor(
                    tmpi[:, 0:nh], tmpi[:, 0:nh], nib9[:, 0:nh], op=ALU.bitwise_or),
                    waits=[t1b])
                t2 = p.op("vector", lambda e, par=par, nh=nh: e.tensor_copy(
                    u16v[par][:, 0:nh], tmpi[:, 0:nh]), waits=[t2a, act_done[par]])
                tg = None
                for gg in range(ngrp):
                    tg = p.op("scalar", lambda e, gg=gg, par=par: e.activation(
                        out=wval[par][:, gg * 128:(gg + 1) * 128],
                        in_=u16v[par][:, gg * 128:(gg + 1) * 128].bitcast(F16),
                        func=AF.Copy, bias=0.0, scale=sclf[par][:, gg:gg + 1]),
                        waits=[t2, tf2, st_par[par]] if gg == 0 else [], tag=(gg == ngrp - 1))
                act_done[par] = tg
                st_tok = p.dma("sync", out_d[oc * 128:(oc + 1) * 128, :], wval[par][:, 0:nh],
                               waits=[tg])
                st_par[par] = st_tok
                pk_tok[par] = tun
            return st_tok

        t_wgu = dequant(88, 256, D, inp["L3f_gup"], inp["L3f_gusT"], wgu_d)
        gu_stores = [st_par[0], st_par[1]]
        tr_tok = [None, None]
        wgu_toks = []
        for hc in range(16):
            par = hc % 2
            tt = p.dmat("scalar", trb[par][:, 0:11264], wgu_d[:, hc * 128:(hc + 1) * 128],
                        waits=gu_stores + [tr_tok[par]])
            ts_ = p.dma("scalar", wguT_d[hc * 128:(hc + 1) * 128, :], trb[par][:, 0:11264],
                        waits=[tt])
            tr_tok[par] = ts_
            wgu_toks.append(ts_)
        t_wdn = dequant(16, 704, IDIM, inp["L3f_dnp"], inp["L3f_dnsT"], wdn_d)
        dn_stores = [st_par[0], st_par[1]]
        wdn_toks = []
        for ic in range(44):
            par = ic % 2
            tt = p.dmat("scalar", trb[par][:, 0:2048], wdn_d[:, ic * 128:(ic + 1) * 128],
                        waits=dn_stores + wgu_toks + [tr_tok[par]])
            ts_ = p.dma("scalar", wdnT_d[ic * 128:(ic + 1) * 128, :], trb[par][:, 0:2048],
                        waits=[tt])
            tr_tok[par] = ts_
            wdn_toks.append(ts_)
        A.pop()

    A.push()
    wg_sb = [A.carve(BF16, [128, 16, 128]) for _ in range(2)]
    wu_sb = [A.carve(BF16, [128, 16, 128]) for _ in range(2)]
    hT = A.carve(BF16, [128, 44, 512])
    gs = A.carve(BF16, [128, 512])
    wdsl = [A.carve(BF16, [128, 512]) for _ in range(2)]
    ysb = A.carve(F32, [128, 4, D])
    z_sb = A.carve(F32, [128, D])
    xc_sb = A.carve(F32, [128, D])
    lnscr = A.carve(BF16, [128, D])
    ccol = A.carve(F32, [128, 16])
    wl_tok = [None, None]
    wd_tok = [None, None]
    t_tile = None
    gs_tok = None
    hT_rd = None
    f3outs = []
    for th in range(2):
        for ocp in range(44):
            par = ocp % 2
            tg_ = p.dma("sync", wg_sb[par][:, :, :],
                        wguT_d.rearrange("(c p) o -> p c o", p=128)[:, :, ocp * 128:(ocp + 1) * 128],
                        waits=wgu_toks + wdn_toks + [wl_tok[par]])
            tu_ = p.dma("sync", wu_sb[par][:, :, :],
                        wguT_d.rearrange("(c p) o -> p c o", p=128)[:, :, (44 + ocp) * 128:(45 + ocp) * 128],
                        waits=[])
            for c in range(16):
                w = [tg_, t_xt, PF.get(0)] if c == 0 else []
                p.op("tensor", lambda e, c=c, th=th, par=par: e.matmul(
                    PS[0][:, :], wg_sb[par][:, c, :], xT[:, c, th * 512:(th + 1) * 512],
                    start=(c == 0), stop=(c == 15)), waits=w, tag=(c == 15))
            tlg = p.tok("T")
            for c in range(16):
                w = [tu_, t_xt, PF.get(1)] if c == 0 else []
                p.op("tensor", lambda e, c=c, th=th, par=par: e.matmul(
                    PS[1][:, :], wu_sb[par][:, c, :], xT[:, c, th * 512:(th + 1) * 512],
                    start=(c == 0), stop=(c == 15)), waits=w, tag=(c == 15))
            tlu = p.tok("T")
            ta = p.op("scalar", lambda e: e.activation(out=gs, in_=PS[0][:, :], func=AF.Silu,
                                                       bias=0.0, scale=1.0),
                      waits=[tlg, gs_tok])
            PF[0] = ta
            tb = p.op("vector", lambda e, ocp=ocp: e.tensor_tensor(
                hT[:, ocp, :], gs, PS[1][:, :], op=ALU.mult), waits=[ta, tlu, hT_rd])
            PF[1] = tb
            gs_tok = tb
            wl_tok[par] = tb
        t_h = p.tok("V")
        ev_toks = []
        for dc in range(4):
            for ic in range(44):
                par = ic % 2
                tw_ = p.dma("sync", wdsl[par],
                            wdnT_d[ic * 128:(ic + 1) * 128, dc * 512:(dc + 1) * 512],
                            waits=wdn_toks + [wd_tok[par]])
                for tt in range(4):
                    w = [tw_, t_h, PF.get(4 + tt)] if ic == 0 else ([tw_] if tt == 0 else [])
                    p.op("tensor", lambda e, ic=ic, tt=tt, dc=dc, par=par: e.matmul(
                        PS[4 + tt][:, :], hT[:, ic, tt * 128:(tt + 1) * 128], wdsl[par],
                        start=(ic == 0), stop=(ic == 43)), waits=w, tag=(tt == 3))
                if ic == 43:
                    tl_ = p.tok("T")
                wd_tok[par] = p.tok("T")
            for tt in range(4):
                tev = p.op("vector", lambda e, tt=tt, dc=dc: e.tensor_copy(
                    ysb[:, tt, dc * 512:(dc + 1) * 512], PS[4 + tt][:, :]), waits=[tl_])
                PF[4 + tt] = tev
                ev_toks.append(tev)
        hT_rd = ev_toks[-1]
        for tt in range(4):
            gi = th * 4 + tt
            tq = p.op("scalar", lambda e, tt=tt: e.activation(
                out=lnscr, in_=ysb[:, tt, :], func=AF.Square, bias=0.0, scale=1.0,
                accum_out=ccol[:, 0:1]), waits=ev_toks + [t_tile])
            tsq = p.op("scalar", lambda e: e.activation(out=ccol[:, 1:2], in_=ccol[:, 0:1],
                                                        func=AF.Sqrt, bias=1e-6, scale=1.0 / D),
                       waits=[tq])
            trc = p.op("vector", lambda e: e.reciprocal(ccol[:, 2:3], ccol[:, 1:2]), waits=[tsq])
            tlx = p.dma("sync", xc_sb, stream_in_ap[gi * 128:(gi + 1) * 128, :],
                        waits=(s.get("stream_ready") or []) + [t_tile])
            tz = p.op("vector", lambda e, tt=tt: e.scalar_tensor_tensor(
                z_sb, ysb[:, tt, :], ccol[:, 2:3], xc_sb, op0=ALU.mult, op1=ALU.add),
                waits=[trc, tlx])
            t_tile = emit_ln_tile(p, s, z_sb, xc_sb, lnscr, ccol,
                                  stream_out_ap[gi * 128:(gi + 1) * 128, :], waits=[tz], cslot=8)
            f3outs.append(t_tile)
    A.pop()
    A.pop()
    return f3outs


def emit_ffn1_block(p, s, stream_in_ap, stream_out_ap):
    A = s["arena"]
    A.push()
    inp = s["inp"]
    PS = s["PS"]; PSB = s["PSB"]; PF = p.psum_free
    srd = s.get("stream_ready") or []
    if not isinstance(srd, list):
        srd = [srd]
    t_bar = [p.tok("T"), p.tok("V"), p.tok("S")] + srd

    xT = A.carve(BF16, [128, 16, TC])
    xbf = A.carve(BF16, [128, D])
    accum = A.carve(F32, [128, NTT, D])
    wgT_sb = A.carve(BF16, [128, 16, 32])
    m_all = A.carve(F32, [128, NTT, 32])
    lgE = A.carve(F32, [128, 32])
    cur = A.carve(F32, [128, 32])
    sel = A.carve(F32, [128, 32])
    eqt = A.carve(F32, [128, 32])
    ccol = A.carve(F32, [128, 16])
    w1f = [A.carve(BF16, [128, 16, 128]) for _ in range(2)]
    v1f = [A.carve(BF16, [128, 16, 128]) for _ in range(2)]
    w2d = [A.carve(BF16, [128, 6, 512]) for _ in range(2)]
    hT = A.carve(BF16, [128, 6, TC])
    gs = A.carve(BF16, [128, 512])
    z_sb = A.carve(F32, [128, D])
    xc_sb = A.carve(F32, [128, D])
    lnscr = A.carve(BF16, [128, D])

    t_xt = emit_xt_build(p, s, stream_in_ap, xT, xbf, waits=t_bar + [s.get("stream_ready")])
    t_wg = p.dma("gpsimd", wgT_sb[:, :, :], inp["L1f_wgT"].rearrange("(c p) e -> p c e", p=128),
                 waits=t_bar)

    t_z = p.op("vector", lambda e: e.memset(accum[:, :, :].rearrange("p a b -> p (a b)"), 0.0),
               waits=t_bar)

    t_m = None
    for i in range(NTT):
        for c in range(16):
            w = [t_wg, t_xt, PF.get(6)] if c == 0 else []
            p.op("tensor", lambda e, c=c, i=i: e.matmul(
                PS[6][:, 0:32], xT[:, c, i * 128:(i + 1) * 128], wgT_sb[:, c, :],
                start=(c == 0), stop=(c == 15)), waits=w, tag=(c == 15))
        tl = p.tok("T")
        t1 = p.op("vector", lambda e: e.reduce_max(ccol[:, 0:1], PS[6][:, 0:32], axis=AX.X),
                  waits=[tl, t_m])
        t2 = p.op("vector", lambda e: e.tensor_scalar(ccol[:, 1:2], ccol[:, 0:1], -1.0, None,
                                                      op0=ALU.mult), waits=[t1])
        t3 = p.op("scalar", lambda e: e.activation(out=lgE, in_=PS[6][:, 0:32], func=AF.Exp,
                                                   bias=ccol[:, 1:2], scale=1.0),
                  waits=[t2, t_m])
        PF[6] = t3
        t4 = p.op("vector", lambda e: e.tensor_copy(cur, lgE), waits=[t3])
        t5 = p.op("vector", lambda e: e.memset(sel, 0.0), waits=[t_m, t4])
        tp = [t4, t5]
        for r in range(8):
            ta = p.op("vector", lambda e: e.reduce_max(ccol[:, 2:3], cur, axis=AX.X), waits=tp)
            tb = p.op("vector", lambda e: e.tensor_scalar(eqt, cur, ccol[:, 2:3], None,
                                                          op0=ALU.is_ge), waits=[ta])
            tc = p.op("vector", lambda e: e.tensor_tensor(sel, sel, eqt, op=ALU.add), waits=[tb])
            td = p.op("vector", lambda e: e.tensor_scalar(eqt, eqt, 1e30, None, op0=ALU.mult),
                      waits=[tb, tc])
            tp = [p.op("vector", lambda e: e.tensor_tensor(cur, cur, eqt, op=ALU.subtract),
                       waits=[td])]
        t6 = p.op("vector", lambda e: e.tensor_tensor(cur, sel, lgE, op=ALU.mult), waits=tp)
        t7 = p.op("vector", lambda e: e.reduce_sum(ccol[:, 3:4], cur, axis=AX.X), waits=[t6])
        t8 = p.op("vector", lambda e: e.reciprocal(ccol[:, 4:5], ccol[:, 3:4]), waits=[t7])
        t_m = p.op("vector", lambda e, i=i: e.tensor_scalar(m_all[:, i, :], cur, ccol[:, 4:5],
                                                            None, op0=ALU.mult), waits=[t8])

    w1_ap = inp["L1f_w1"].rearrange("e (c p) f -> e p c f", p=128)
    v1_ap = inp["L1f_v1"].rearrange("e (c p) f -> e p c f", p=128)
    w2_ap = inp["L1f_w2"].rearrange("e (c p) d -> e p c d", p=128)
    wtok = {}
    hT_rd = None
    gs_tok = None
    ev_tok = None
    for e_i in range(32):
        for f in range(6):
            par = f % 2
            t_w1 = p.dma("gpsimd", w1f[par][:, :, :],
                         w1_ap[e_i, :, :, f * 128:(f + 1) * 128],
                         waits=[wtok.get(("w1", par))] + t_bar)
            t_v1 = p.dma("gpsimd", v1f[par][:, :, :],
                         v1_ap[e_i, :, :, f * 128:(f + 1) * 128],
                         waits=[wtok.get(("v1", par))])
            for th in range(2):
                pbg = 0 + (th % 2)
                pbu = 2 + (th % 2)
                for c in range(16):
                    w = [t_w1, t_xt, PF.get(pbg)] if c == 0 else []
                    p.op("tensor", lambda e, c=c, f=f, th=th, par=par, pbg=pbg: e.matmul(
                        PS[pbg][:, :], w1f[par][:, c, :], xT[:, c, th * 512:(th + 1) * 512],
                        start=(c == 0), stop=(c == 15)), waits=w, tag=(c == 15))
                tlg = p.tok("T")
                for c in range(16):
                    w = [t_v1, PF.get(pbu)] if c == 0 else []
                    p.op("tensor", lambda e, c=c, f=f, th=th, par=par, pbu=pbu: e.matmul(
                        PS[pbu][:, :], v1f[par][:, c, :], xT[:, c, th * 512:(th + 1) * 512],
                        start=(c == 0), stop=(c == 15)), waits=w, tag=(c == 15))
                tlu = p.tok("T")
                wtok[("w1", par)] = tlg
                wtok[("v1", par)] = tlu
                ta = p.op("scalar", lambda e, pbg=pbg: e.activation(
                    out=gs, in_=PS[pbg][:, :], func=AF.Silu, bias=0.0, scale=1.0),
                    waits=[tlg, gs_tok])
                PF[pbg] = ta
                tb = p.op("vector", lambda e, f=f, th=th, pbu=pbu: e.tensor_tensor(
                    hT[:, f, th * 512:(th + 1) * 512], gs, PS[pbu][:, :], op=ALU.mult),
                    waits=[ta, tlu, hT_rd])
                PF[pbu] = tb
                gs_tok = tb
        t_h = p.tok("V")
        for dc in range(4):
            par = dc % 2
            t_w2 = p.dma("gpsimd", w2d[par][:, :, :],
                         w2_ap[e_i, :, :, dc * 512:(dc + 1) * 512],
                         waits=[wtok.get(("w2", par))] + t_bar)
            for tt in range(NTT):
                pb = 4 + (tt % 2)
                for f in range(6):
                    w = [t_w2, t_h, PF.get(pb)] if f == 0 else []
                    p.op("tensor", lambda e, f=f, tt=tt, dc=dc, par=par, pb=pb: e.matmul(
                        PS[pb][:, :], hT[:, f, tt * 128:(tt + 1) * 128], w2d[par][:, f, :],
                        start=(f == 0), stop=(f == 5)), waits=w, tag=(f == 5))
                tly = p.tok("T")
                ev_tok = p.op("vector", lambda e, tt=tt, dc=dc, e_i=e_i, pb=pb:
                              e.scalar_tensor_tensor(
                                  accum[:, tt, dc * 512:(dc + 1) * 512], PS[pb][:, :],
                                  m_all[:, tt, e_i:e_i + 1],
                                  accum[:, tt, dc * 512:(dc + 1) * 512],
                                  op0=ALU.mult, op1=ALU.add),
                              waits=[tly, t_m, t_z, ev_tok])
                PF[pb] = ev_tok
            wtok[("w2", par)] = p.tok("T")
        hT_rd = p.tok("T")

    t_tile = None
    outs = []
    for i in range(NTT):
        tlx = p.dma("sync", xc_sb, stream_in_ap[i * 128:(i + 1) * 128, :], waits=srd + [t_tile])
        tz2 = p.op("vector", lambda e, i=i: e.tensor_tensor(z_sb, accum[:, i, :], xc_sb,
                                                            op=ALU.add),
                   waits=[tlx, ev_tok, t_tile])
        t_tile = emit_ln_tile(p, s, z_sb, xc_sb, lnscr, ccol,
                              stream_out_ap[i * 128:(i + 1) * 128, :], waits=[tz2], cslot=8)
        outs.append(t_tile)
    A.pop()
    return outs


FFN_EMITTERS = {}


def build_ffn_program(li):
    nc = bass.Bass("TRN2", target_bir_lowering=False)
    inp = {}
    inp["stream"] = nc.dram_tensor("stream", [TC, D], F32, kind="ExternalInput").ap()
    if li == 0:
        inp["L0f_swT"] = nc.dram_tensor("swT", [D, 64], F32, kind="ExternalInput").ap()
        inp["L0f_muT"] = nc.dram_tensor("muT", [D, 8], F32, kind="ExternalInput").ap()
        inp["L0f_updT"] = nc.dram_tensor("updT", [D, 64], F32, kind="ExternalInput").ap()
        inp["L0f_debT"] = nc.dram_tensor("debT", [D, 8], F32, kind="ExternalInput").ap()
        inp["L0f_dgnT"] = nc.dram_tensor("dgnT", [64, D], F32, kind="ExternalInput").ap()
        inp["L0f_bias"] = nc.dram_tensor("bias", [8, D], F32, kind="ExternalInput").ap()
        inp["L0f_slope"] = nc.dram_tensor("slope", [1, 8], F32, kind="ExternalInput").ap()
        inp["L0f_center"] = nc.dram_tensor("center", [1, 8], F32, kind="ExternalInput").ap()
    elif li == 1:
        inp["L1f_wgT"] = nc.dram_tensor("wgT", [D, 32], F32, kind="ExternalInput").ap()
        inp["L1f_w1"] = nc.dram_tensor("w1", [32, D, 768], F32, kind="ExternalInput").ap()
        inp["L1f_v1"] = nc.dram_tensor("v1", [32, D, 768], F32, kind="ExternalInput").ap()
        inp["L1f_w2"] = nc.dram_tensor("w2", [32, 768, D], F32, kind="ExternalInput").ap()
    elif li == 2:
        inp["L2f_WcT"] = nc.dram_tensor("WcT", [D, 4096], F32, kind="ExternalInput").ap()
        inp["L2f_genes"] = nc.dram_tensor("genes", [4096, 256], F32, kind="ExternalInput").ap()
        inp["L2f_WdT"] = nc.dram_tensor("WdT", [256, D], F32, kind="ExternalInput").ap()
        inp["L2f_WuT"] = nc.dram_tensor("WuT", [256, D], F32, kind="ExternalInput").ap()
        inp["L2f_nw"] = nc.dram_tensor("nw", [1, D], F32, kind="ExternalInput").ap()
        inp["L2f_ts"] = nc.dram_tensor("ts", [1, 1], F32, kind="ExternalInput").ap()
        inp["L2f_scale"] = nc.dram_tensor("scale", [1, 1], F32, kind="ExternalInput").ap()
    else:
        inp["L3f_gup"] = nc.dram_tensor("gup", [11264, 256], I32, kind="ExternalInput").ap()
        inp["L3f_gusT"] = nc.dram_tensor("gusT", [11264, 16], F16, kind="ExternalInput").ap()
        inp["L3f_dnp"] = nc.dram_tensor("dnp", [D, 704], I32, kind="ExternalInput").ap()
        inp["L3f_dnsT"] = nc.dram_tensor("dnsT", [D, 44], F16, kind="ExternalInput").ap()
    inp["identb"] = nc.dram_tensor("identb", [128, 128], BF16, kind="ExternalInput").ap()
    inp["masksT"] = [nc.dram_tensor(f"maskT{i}", [128, 512], BF16, kind="ExternalInput").ap()
                     for i in range(4)]
    out_ap = nc.dram_tensor("stream_out", [TC, D], F32, kind="ExternalOutput").ap()
    if DEBUG_ATTN:
        dbgf = nc.dram_tensor("dbgf", [TC, 64], F32, kind="ExternalOutput").ap()
    else:
        dbgf = None
    dram = {}
    if li == 2:
        dram["logits"] = nc.dram_tensor("logits_d", [TC, 4096], F32).ap()
    if li == 3:
        dram["wgu"] = nc.dram_tensor("wgu_d", [11264, D], BF16).ap()
        dram["wguT"] = nc.dram_tensor("wguT_d", [D, 11264], BF16).ap()
        dram["wdn"] = nc.dram_tensor("wdn_d", [D, 5632], BF16).ap()
        dram["wdnT"] = nc.dram_tensor("wdnT_d", [5632, D], BF16).ap()

    p = Prog()
    with contextlib.ExitStack() as stack:
        arena_t = stack.enter_context(nc.sbuf_tensor("arena", [128, NW32], F32))
        PS = [stack.enter_context(nc.psum_tensor(f"ps{i}", [128, 512], F32)) for i in range(8)]
        for k in p.cnt:
            p.sems[k] = stack.enter_context(nc.semaphore(f"sem_{k}"))
        s = {
            "_nc": nc,
            "arena": Arena(arena_t[:], NW32 * 4),
            "PS": [x[:] for x in PS],
            "PSB": [x[:].bitcast(BF16) for x in PS],
            "inp": inp,
            "dram": dram,
        }
        with nc.Block() as block:
            make_consts(p, s)
            s["dbgf"] = dbgf
            t_out = FFN_EMITTERS[li](p, s, inp["stream"], out_ap)
            for t in (t_out if isinstance(t_out, list) else [t_out]):
                p._wait("sync", t)
            p.finalize(nc, block)
    return nc


def ffn_in_maps(li, inputs, streams):
    identb, masksT = host_consts()
    maps = []
    for c in range(8):
        m = {"stream": streams[c], "identb": identb}
        for i in range(4):
            m[f"maskT{i}"] = masksT[i]
        if li == 0:
            sw = inputs["L0_ffn_select_weight"]
            m["swT"] = np.ascontiguousarray(sw.transpose(1, 0, 2).reshape(D, 64))
            m["muT"] = np.ascontiguousarray(inputs["L0_ffn_select_mean_diff"].T)
            m["updT"] = np.ascontiguousarray(
                inputs["L0_ffn_lora_update"].transpose(1, 0, 2).reshape(D, 64))
            m["debT"] = np.ascontiguousarray(inputs["L0_ffn_debias_w"].T)
            m["dgnT"] = np.ascontiguousarray(
                inputs["L0_ffn_lora_degen"].transpose(0, 2, 1).reshape(64, D))
            m["bias"] = np.ascontiguousarray(inputs["L0_ffn_bias_w"])
            m["slope"] = np.ascontiguousarray(inputs["L0_ffn_imp_slope"].reshape(1, 8))
            m["center"] = np.ascontiguousarray(inputs["L0_ffn_imp_center"].reshape(1, 8))
        elif li == 1:
            m["wgT"] = np.ascontiguousarray(inputs["L1_ffn_wg"].T)
            m["w1"] = inputs["L1_ffn_w1"]
            m["v1"] = inputs["L1_ffn_v1"]
            m["w2"] = inputs["L1_ffn_w2"]
        elif li == 2:
            m["WcT"] = np.ascontiguousarray(inputs["L2_ffn_Wc"].T)
            m["genes"] = inputs["L2_ffn_genes"]
            m["WdT"] = np.ascontiguousarray(inputs["L2_ffn_Wd"].T)
            m["WuT"] = np.ascontiguousarray(inputs["L2_ffn_Wu"].T)
            m["nw"] = np.ascontiguousarray(inputs["L2_ffn_norm_w"].reshape(1, D))
            m["ts"] = np.ascontiguousarray(inputs["L2_ffn_temperature"].reshape(1, 1))
            m["scale"] = np.ascontiguousarray(inputs["L2_ffn_scale"].reshape(1, 1))
        else:
            m["gup"] = inputs["L3_ffn_gate_up_packed"]
            m["gusT"] = np.ascontiguousarray(inputs["L3_ffn_gate_up_scales"].T)
            m["dnp"] = inputs["L3_ffn_down_packed"]
            m["dnsT"] = np.ascontiguousarray(inputs["L3_ffn_down_scales"].T)
        maps.append(m)
    return maps


def run_ffn_block(li, inputs, streams):
    if ("ffn", li) not in _prog_cache:
        _prog_cache[("ffn", li)] = build_ffn_program(li)
    nc = _prog_cache[("ffn", li)]
    maps = ffn_in_maps(li, inputs, streams)
    res = run_bass_kernel_spmd(nc, maps, core_ids=list(range(8)))
    return [res.results[c]["stream_out"] for c in range(8)]


def declare_attn_inputs(nc, li, inp, pfx):
    cfg = ATTN_CFG[li]
    nkv = HPC if cfg["kv_per_head"] else 1
    inp[f"L{li}_wq"] = nc.dram_tensor(pfx + "wq", [D, 512], F32, kind="ExternalInput").ap()
    inp[f"L{li}_wk"] = nc.dram_tensor(pfx + "wk", [D, 128 * nkv], F32, kind="ExternalInput").ap()
    inp[f"L{li}_wv"] = nc.dram_tensor(pfx + "wv", [D, 128 * nkv], F32, kind="ExternalInput").ap()
    inp[f"L{li}_wo"] = nc.dram_tensor(pfx + "wo", [512, D], F32, kind="ExternalInput").ap()
    if cfg["qkv_bias"]:
        for nm in ("bq_row", "bk_row", "bv_row"):
            inp[f"L{li}_{nm}"] = nc.dram_tensor(pfx + nm, [1, 512], F32, kind="ExternalInput").ap()
    if cfg["bo"]:
        inp[f"L{li}_bo_row"] = nc.dram_tensor(pfx + "bo_row", [1, D], F32, kind="ExternalInput").ap()
    if cfg["alibi"]:
        inp[f"L{li}_alibiT"] = nc.dram_tensor(pfx + "alibiT", [128, 128], F32, kind="ExternalInput").ap()
        inp[f"L{li}_alibi_dneg"] = nc.dram_tensor(pfx + "alibi_dneg", [1, HPC * T], F32, kind="ExternalInput").ap()
    if cfg["rope"]:
        inp[f"L{li}_ropeA"] = nc.dram_tensor(pfx + "ropeA", [128, T], BF16, kind="ExternalInput").ap()
        inp[f"L{li}_ropeB"] = nc.dram_tensor(pfx + "ropeB", [128, T], BF16, kind="ExternalInput").ap()


def declare_ffn_inputs(nc, li, inp, pfx):
    if li == 0:
        inp["L0f_swT"] = nc.dram_tensor(pfx + "swT", [D, 64], F32, kind="ExternalInput").ap()
        inp["L0f_muT"] = nc.dram_tensor(pfx + "muT", [D, 8], F32, kind="ExternalInput").ap()
        inp["L0f_updT"] = nc.dram_tensor(pfx + "updT", [D, 64], F32, kind="ExternalInput").ap()
        inp["L0f_debT"] = nc.dram_tensor(pfx + "debT", [D, 8], F32, kind="ExternalInput").ap()
        inp["L0f_dgnT"] = nc.dram_tensor(pfx + "dgnT", [64, D], F32, kind="ExternalInput").ap()
        inp["L0f_bias"] = nc.dram_tensor(pfx + "bias", [8, D], F32, kind="ExternalInput").ap()
        inp["L0f_slope"] = nc.dram_tensor(pfx + "slope", [1, 8], F32, kind="ExternalInput").ap()
        inp["L0f_center"] = nc.dram_tensor(pfx + "center", [1, 8], F32, kind="ExternalInput").ap()
    elif li == 1:
        inp["L1f_wgT"] = nc.dram_tensor(pfx + "wgT", [D, 32], F32, kind="ExternalInput").ap()
        inp["L1f_w1"] = nc.dram_tensor(pfx + "w1", [32, D, 768], F32, kind="ExternalInput").ap()
        inp["L1f_v1"] = nc.dram_tensor(pfx + "v1", [32, D, 768], F32, kind="ExternalInput").ap()
        inp["L1f_w2"] = nc.dram_tensor(pfx + "w2", [32, 768, D], F32, kind="ExternalInput").ap()
    elif li == 2:
        inp["L2f_WcT"] = nc.dram_tensor(pfx + "WcT", [D, 4096], F32, kind="ExternalInput").ap()
        inp["L2f_genes"] = nc.dram_tensor(pfx + "genes", [4096, 256], F32, kind="ExternalInput").ap()
        inp["L2f_WdT"] = nc.dram_tensor(pfx + "WdT", [256, D], F32, kind="ExternalInput").ap()
        inp["L2f_WuT"] = nc.dram_tensor(pfx + "WuT", [256, D], F32, kind="ExternalInput").ap()
        inp["L2f_nw"] = nc.dram_tensor(pfx + "nw", [1, D], F32, kind="ExternalInput").ap()
        inp["L2f_ts"] = nc.dram_tensor(pfx + "ts", [1, 1], F32, kind="ExternalInput").ap()
        inp["L2f_scale"] = nc.dram_tensor(pfx + "scale", [1, 1], F32, kind="ExternalInput").ap()
    else:
        inp["L3f_gup"] = nc.dram_tensor(pfx + "gup", [11264, 256], I32, kind="ExternalInput").ap()
        inp["L3f_gusT"] = nc.dram_tensor(pfx + "gusT", [11264, 16], F16, kind="ExternalInput").ap()
        inp["L3f_dnp"] = nc.dram_tensor(pfx + "dnp", [D, 704], I32, kind="ExternalInput").ap()
        inp["L3f_dnsT"] = nc.dram_tensor(pfx + "dnsT", [D, 44], F16, kind="ExternalInput").ap()


def make_attn_dram(nc, sfx=""):
    return {
        "ag_in": nc.dram_tensor("ag_in" + sfx, [TC, D], BF16).ap(),
        "xg": nc.dram_tensor("xg" + sfx, [T, D], BF16).ap(),
        "xgt": nc.dram_tensor("xgt" + sfx, [D, T], BF16).ap(),
        "ytd": nc.dram_tensor("ytd" + sfx, [512, T], BF16).ap(),
        "rs_in": nc.dram_tensor("rs_in" + sfx, [T, D], BF16).ap(),
        "rs_out": nc.dram_tensor("rs_out" + sfx, [TC, D], BF16).ap(),
    }


def build_full_program():
    nc = bass.Bass("TRN2", target_bir_lowering=False)
    inp = {}
    inp["stream"] = nc.dram_tensor("stream", [TC, D], F32, kind="ExternalInput").ap()
    for li in range(4):
        declare_attn_inputs(nc, li, inp, f"a{li}_")
        declare_ffn_inputs(nc, li, inp, f"f{li}_")
    inp["identb"] = nc.dram_tensor("identb", [128, 128], BF16, kind="ExternalInput").ap()
    inp["masksT"] = [nc.dram_tensor(f"maskT{i}", [128, 512], BF16, kind="ExternalInput").ap()
                     for i in range(4)]
    out_ap = nc.dram_tensor("stream_out", [TC, D], F32, kind="ExternalOutput").ap()

    dram = make_attn_dram(nc)
    dram["logits"] = nc.dram_tensor("logits_d", [TC, 4096], F32).ap()
    dram["wgu"] = nc.dram_tensor("wgu_d", [11264, D], BF16).ap()
    dram["wguT"] = nc.dram_tensor("wguT_d", [D, 11264], BF16).ap()
    dram["wdn"] = nc.dram_tensor("wdn_d", [D, 5632], BF16).ap()
    dram["wdnT"] = nc.dram_tensor("wdnT_d", [5632, D], BF16).ap()
    streams = [inp["stream"]] + [nc.dram_tensor(f"sbuf{k}", [TC, D], F32).ap() for k in range(7)] + [out_ap]

    p = Prog()
    with contextlib.ExitStack() as stack:
        arena_t = stack.enter_context(nc.sbuf_tensor("arena", [128, NW32], F32))
        PS = [stack.enter_context(nc.psum_tensor(f"ps{i}", [128, 512], F32)) for i in range(8)]
        for k in p.cnt:
            p.sems[k] = stack.enter_context(nc.semaphore(f"sem_{k}"))
        s = {
            "_nc": nc,
            "arena": Arena(arena_t[:], NW32 * 4),
            "PS": [x[:] for x in PS],
            "PSB": [x[:].bitcast(BF16) for x in PS],
            "inp": inp,
            "dram": dram,
            "dbg": None,
            "dbgf": None,
        }
        with nc.Block() as block:
            make_consts(p, s)
            stp_prev = None
            PSITE = {0: 1, 1: 1, 2: 1, 3: 1}
            for li in range(4):
                s["arena"].push()
                stp = F3Stepper(p, s, li, other=stp_prev, per_site=PSITE[li])
                t_out = emit_attn_block(p, s, li, streams[2 * li], streams[2 * li + 1],
                                        filler=stp.step)
                stp.finish()
                extra = list(stp.st_par) + [stp.tr_tok]
                if li == 3:
                    s["f3_pre"] = (stp.wgu_toks, stp.wdn_toks)
                stp_prev = stp
                s["arena"].pop()
                t_out = t_out + [t for t in extra if t is not None]
                s["stream_ready"] = t_out
                t_out = FFN_EMITTERS[li](p, s, streams[2 * li + 1], streams[2 * li + 2])
                s["stream_ready"] = t_out
            for t in t_out:
                p._wait("sync", t)
            p.finalize(nc, block)
    return nc


def full_in_maps(inputs, streams):
    identb, masksT = host_consts()
    amaps = {li: attn_in_maps(li, inputs, streams) for li in range(4)}
    fmaps = {li: ffn_in_maps(li, inputs, streams) for li in range(4)}
    maps = []
    for c in range(8):
        m = {"stream": streams[c], "identb": identb}
        for i in range(4):
            m[f"maskT{i}"] = masksT[i]
        for li in range(4):
            for k, v in amaps[li][c].items():
                if k in ("stream", "identb") or k.startswith("maskT"):
                    continue
                m[f"a{li}_{k}"] = v
            for k, v in fmaps[li][c].items():
                if k in ("stream", "identb") or k.startswith("maskT"):
                    continue
                m[f"f{li}_{k}"] = v
        maps.append(m)
    return maps


def run_full(inputs, streams):
    if "full" not in _prog_cache:
        _prog_cache["full"] = build_full_program()
    nc = _prog_cache["full"]
    maps = full_in_maps(inputs, streams)
    res = run_bass_kernel_spmd(nc, maps, core_ids=list(range(8)))
    return [res.results[c]["stream_out"] for c in range(8)]


def build_attn_program(li):
    nc = bass.Bass("TRN2", target_bir_lowering=False)
    cfg = ATTN_CFG[li]
    nkv = HPC if cfg["kv_per_head"] else 1
    inp = {}
    inp["stream"] = nc.dram_tensor("stream", [TC, D], F32, kind="ExternalInput").ap()
    inp[f"L{li}_wq"] = nc.dram_tensor("wq", [D, 512], F32, kind="ExternalInput").ap()
    inp[f"L{li}_wk"] = nc.dram_tensor("wk", [D, 128 * nkv], F32, kind="ExternalInput").ap()
    inp[f"L{li}_wv"] = nc.dram_tensor("wv", [D, 128 * nkv], F32, kind="ExternalInput").ap()
    inp[f"L{li}_wo"] = nc.dram_tensor("wo", [512, D], F32, kind="ExternalInput").ap()
    if cfg["qkv_bias"]:
        for nm in ("bq_row", "bk_row", "bv_row"):
            inp[f"L{li}_{nm}"] = nc.dram_tensor(nm, [1, 512], F32, kind="ExternalInput").ap()
    if cfg["bo"]:
        inp[f"L{li}_bo_row"] = nc.dram_tensor("bo_row", [1, D], F32, kind="ExternalInput").ap()
    inp["identb"] = nc.dram_tensor("identb", [128, 128], BF16, kind="ExternalInput").ap()
    inp["masksT"] = [nc.dram_tensor(f"maskT{i}", [128, 512], BF16, kind="ExternalInput").ap()
                     for i in range(4)]
    if cfg["alibi"]:
        inp[f"L{li}_alibiT"] = nc.dram_tensor("alibiT", [128, 128], F32, kind="ExternalInput").ap()
        inp[f"L{li}_alibi_dneg"] = nc.dram_tensor("alibi_dneg", [1, HPC * T], F32, kind="ExternalInput").ap()
    if cfg["rope"]:
        inp[f"L{li}_ropeA"] = nc.dram_tensor("ropeA", [128, T], BF16, kind="ExternalInput").ap()
        inp[f"L{li}_ropeB"] = nc.dram_tensor("ropeB", [128, T], BF16, kind="ExternalInput").ap()
    out_ap = nc.dram_tensor("stream_out", [TC, D], F32, kind="ExternalOutput").ap()
    if DEBUG_ATTN:
        dbg = {
            "dbg_ytd": nc.dram_tensor("dbg_ytd", [512, T], BF16, kind="ExternalOutput").ap(),
            "dbg_rsout": nc.dram_tensor("dbg_rsout", [TC, D], F32, kind="ExternalOutput").ap(),
            "dbg_qt": nc.dram_tensor("dbg_qt", [128, 2 * T], BF16, kind="ExternalOutput").ap(),
            "dbg_kt": nc.dram_tensor("dbg_kt", [128, 2 * T], BF16, kind="ExternalOutput").ap(),
            "dbg_va": nc.dram_tensor("dbg_va", [128, 2 * T], BF16, kind="ExternalOutput").ap(),
            "dbg_negm": nc.dram_tensor("dbg_negm", [1, T], BF16, kind="ExternalOutput").ap(),
            "dbg_xgt": nc.dram_tensor("dbg_xgt", [D, T], BF16, kind="ExternalOutput").ap(),
            "dbg_xg": nc.dram_tensor("dbg_xg", [T, D], BF16, kind="ExternalOutput").ap(),
        }
    else:
        dbg = None

    dram = {
        "ag_in": nc.dram_tensor("ag_in", [TC, D], BF16).ap(),
        "xg": nc.dram_tensor("xg", [T, D], BF16).ap(),
        "xgt": nc.dram_tensor("xgt", [D, T], BF16).ap(),
        "ytd": nc.dram_tensor("ytd", [512, T], BF16).ap(),
        "rs_in": nc.dram_tensor("rs_in", [T, D], BF16).ap(),
        "rs_out": nc.dram_tensor("rs_out", [TC, D], BF16).ap(),
    }

    p = Prog()
    with contextlib.ExitStack() as stack:
        arena_t = stack.enter_context(nc.sbuf_tensor("arena", [128, NW32], F32))
        PS = [stack.enter_context(nc.psum_tensor(f"ps{i}", [128, 512], F32)) for i in range(8)]
        for k in p.cnt:
            p.sems[k] = stack.enter_context(nc.semaphore(f"sem_{k}"))
        s = {
            "_nc": nc,
            "arena": Arena(arena_t[:], NW32 * 4),
            "PS": [x[:] for x in PS],
            "PSB": [x[:].bitcast(BF16) for x in PS],
            "inp": inp,
            "dram": dram,
        }
        with nc.Block() as block:
            make_consts(p, s)
            s["dbg"] = dbg
            t_out = emit_attn_block(p, s, li, inp["stream"], out_ap)
            t_out = t_out[-1] if isinstance(t_out, list) else t_out
            if dbg is not None:
                t_out = p.dma("sync", dbg["dbg_ytd"], s["dram"]["ytd"], waits=[t_out])
                t_out = p.dma("sync", dbg["dbg_negm"], s["negm_ap"], waits=[t_out])
                t_out = p.dma("sync", dbg["dbg_xgt"], s["dram"]["xgt"], waits=[t_out])
                t_out = p.dma("sync", dbg["dbg_xg"], s["dram"]["xg"], waits=[t_out])
                t_out = p.dma("sync", dbg["dbg_rsout"], s["dram"]["rs_out"], waits=[t_out])
            p._wait("sync", t_out)
            p.finalize(nc, block)
    return nc


def host_consts():
    identb = np.eye(128, dtype=bf)
    masksT = []
    ki = np.arange(128)[:, None]
    qj = np.arange(512)[None, :]
    for idx in range(4):
        m = np.where(qj >= idx * 128 + ki, 0.0, NEG).astype(np.float32)
        masksT.append(m.astype(bf))
    return identb, masksT


def alibi_consts(heads):
    ratio = 2.0 ** (-8.0 / NH)
    slopes = ratio ** np.arange(1, NH + 1, dtype=np.float32)
    aT = np.zeros((128, 128), np.float32)
    for hi, h in enumerate(heads):
        for ks in range(32):
            aT[:, hi * 32 + ks] = slopes[h] * ((ks * 128 + np.arange(128)) - (T - 1))
    dneg = np.zeros((1, HPC * T), np.float32)
    for hi, h in enumerate(heads):
        dneg[0, hi * T:(hi + 1) * T] = -slopes[h] * (np.arange(T) - (T - 1))
    return aT, dneg


def rope_consts():
    freqs = 1.0 / 10000.0 ** (np.arange(0, HD, 2, dtype=np.float32) / HD)
    ang = np.arange(T, dtype=np.float32)[:, None] * freqs[None, :]
    cos = np.cos(ang).T.astype(np.float32)
    sin = np.sin(ang).T.astype(np.float32)
    A = np.ascontiguousarray(np.concatenate([cos, cos], axis=0)).astype(bf)
    Bm = np.ascontiguousarray(np.concatenate([-sin, sin], axis=0)).astype(bf)
    return A, Bm


def attn_in_maps(li, inputs, streams):
    cfg = ATTN_CFG[li]
    identb, masksT = host_consts()
    if cfg["rope"]:
        ropeA, ropeB = rope_consts()
    maps = []
    for c in range(8):
        r = c % 4
        heads = list(range(4 * r, 4 * r + 4))
        m = {"stream": streams[c], "identb": identb}
        for i in range(4):
            m[f"maskT{i}"] = masksT[i]
        if li == 0:
            Wq, Wk, Wv = inputs["L0_attn_Wq"], inputs["L0_attn_Wk"], inputs["L0_attn_Wv"]
            m["wq"] = np.ascontiguousarray(np.concatenate([Wq[h] for h in heads], axis=1))
            m["wk"] = np.ascontiguousarray(np.concatenate([Wk[h] for h in heads], axis=1))
            m["wv"] = np.ascontiguousarray(np.concatenate([Wv[h] for h in heads], axis=1))
            m["wo"] = np.ascontiguousarray(inputs["L0_attn_Wo"][512 * r:512 * (r + 1), :])
            m["bq_row"] = np.ascontiguousarray(inputs["L0_attn_bq"][heads].reshape(1, 512))
            m["bk_row"] = np.ascontiguousarray(inputs["L0_attn_bk"][heads].reshape(1, 512))
            m["bv_row"] = np.ascontiguousarray(inputs["L0_attn_bv"][heads].reshape(1, 512))
            m["bo_row"] = np.ascontiguousarray(inputs["L0_attn_bo"].reshape(1, D))
        elif li == 1:
            m["wq"] = np.ascontiguousarray(inputs["L1_attn_Wq"][:, 512 * r:512 * (r + 1)])
            m["wk"] = np.ascontiguousarray(inputs["L1_attn_Wk"][:, 512 * r:512 * (r + 1)])
            m["wv"] = np.ascontiguousarray(inputs["L1_attn_Wv"][:, 512 * r:512 * (r + 1)])
            m["wo"] = np.ascontiguousarray(inputs["L1_attn_Wo"][512 * r:512 * (r + 1), :])
            m["bq_row"] = np.ascontiguousarray(inputs["L1_attn_bq"][512 * r:512 * (r + 1)].reshape(1, 512))
            m["bk_row"] = np.ascontiguousarray(inputs["L1_attn_bk"][512 * r:512 * (r + 1)].reshape(1, 512))
            m["bv_row"] = np.ascontiguousarray(inputs["L1_attn_bv"][512 * r:512 * (r + 1)].reshape(1, 512))
            m["bo_row"] = np.ascontiguousarray(inputs["L1_attn_bo"].reshape(1, D))
            m["alibiT"], m["alibi_dneg"] = alibi_consts(heads)
        elif li == 2:
            m["wq"] = np.ascontiguousarray(inputs["L2_attn_Wq"][:, 512 * r:512 * (r + 1)])
            m["wk"] = np.ascontiguousarray(inputs["L2_attn_Wk"])
            m["wv"] = np.ascontiguousarray(inputs["L2_attn_Wv"])
            m["wo"] = np.ascontiguousarray(inputs["L2_attn_Wo"][512 * r:512 * (r + 1), :])
            m["bo_row"] = np.ascontiguousarray(inputs["L2_attn_bo"].reshape(1, D))
        else:
            Wqkv = inputs["L3_attn_Wqkv"]
            m["wq"] = np.ascontiguousarray(Wqkv[:, 512 * r:512 * (r + 1)])
            m["wk"] = np.ascontiguousarray(Wqkv[:, 2048 + 128 * r:2048 + 128 * (r + 1)])
            m["wv"] = np.ascontiguousarray(Wqkv[:, 2560 + 128 * r:2560 + 128 * (r + 1)])
            m["wo"] = np.ascontiguousarray(inputs["L3_attn_Wo"][512 * r:512 * (r + 1), :])
            m["ropeA"] = ropeA
            m["ropeB"] = ropeB
        maps.append(m)
    return maps


def split_streams(x):
    out = []
    for c in range(8):
        g, r = c // 4, c % 4
        out.append(np.ascontiguousarray(x[g, TC * r:TC * (r + 1), :]))
    return out


def join_streams(slices):
    x = np.empty((B, T, D), np.float32)
    for c in range(8):
        g, r = c // 4, c % 4
        x[g, TC * r:TC * (r + 1), :] = np.asarray(slices[c])
    return x


_prog_cache = {}


def run_attn_block(li, inputs, streams):
    if ("attn", li) not in _prog_cache:
        _prog_cache[("attn", li)] = build_attn_program(li)
    nc = _prog_cache[("attn", li)]
    maps = attn_in_maps(li, inputs, streams)
    res = run_bass_kernel_spmd(nc, maps, core_ids=list(range(8)))
    return [res.results[c]["stream_out"] for c in range(8)]


FFN_EMITTERS[0] = emit_ffn0_block
FFN_EMITTERS[2] = emit_ffn2_block
FFN_EMITTERS[3] = emit_ffn3_block
FFN_EMITTERS[1] = emit_ffn1_block


USE_FUSED = _os.environ.get("USE_FUSED", "1") == "1"


def kernel(**inputs):
    streams = split_streams(inputs["L0_attn_x"])
    if USE_FUSED:
        return join_streams(run_full(inputs, streams))
    for li in range(4):
        streams = run_attn_block(li, inputs, streams)
        streams = run_ffn_block(li, inputs, streams)
    return join_streams(streams)
